# version 23
# baseline (speedup 1.0000x reference)
import numpy as np
import ml_dtypes
from contextlib import ExitStack

BF16_NP = ml_dtypes.bfloat16

import concourse.bass as bass
import concourse.mybir as mybir
from concourse.bass_utils import run_bass_kernel_spmd
from concourse.alu_op_type import AluOpType

F32 = mybir.dt.float32
BF16 = mybir.dt.bfloat16
AF = mybir.ActivationFunctionType
AX = mybir.AxisListType

NH, NKV, HD, D = 32, 8, 128, 4096
SCALE = float(HD) ** -0.5
EPS = 1e-5
CHUNKS = {0: [0, 3, 4, 7], 1: [1, 2, 5, 6]}
INV_D = 1.0 / 4096.0


class Ctr:
    def __init__(self, stack, nc, name):
        self.h = stack.enter_context(nc.semaphore(name))
        self.v = 0

    def inc(self, inst, k=1):
        inst.then_inc(self.h, k)
        self.v += k
        return self.v


def build_program(phases=4, nheads=32, do_av=True, do_norm=True, do_tr=True, do_spill=True, do_exp=True, do_mask=True, dump=False):
    nc = bass.Bass("TRN2", target_bir_lowering=False)

    XT = nc.dram_tensor("XT", [4, 128, 32, 512], BF16, kind="ExternalInput").ap()
    XOWN = nc.dram_tensor("XOWN", [128, 32, 1024], F32, kind="ExternalInput").ap()
    WKV = nc.dram_tensor("WKV", [16, 128, 4096], BF16, kind="ExternalInput").ap()
    WQ = nc.dram_tensor("WQ", [32, 128, 4096], BF16, kind="ExternalInput").ap()
    WOD = nc.dram_tensor("WOD", [32, 128, 4096], BF16, kind="ExternalInput").ap()
    FFB = nc.dram_tensor("FFB", [32, 128, 5120], BF16, kind="ExternalInput").ap()
    ARTW = nc.dram_tensor("ARTW", [128, 2560], BF16, kind="ExternalInput").ap()
    MSK = nc.dram_tensor("MSK", [128, 4096], BF16, kind="ExternalInput").ap()
    M1D = nc.dram_tensor("M1D", [16, 1024], F32, kind="ExternalInput").ap()
    M2D = nc.dram_tensor("M2D", [64, 128], F32, kind="ExternalInput").ap()
    BBD = nc.dram_tensor("BBD", [128, 32], F32, kind="ExternalInput").ap()
    IDD = nc.dram_tensor("IDD", [128, 128], F32, kind="ExternalInput").ap()
    OUT = nc.dram_tensor("OUT", [4096, 1024], F32, kind="ExternalOutput").ap()
    OSC = nc.dram_tensor("OSC", [128, 32768], BF16, kind="Internal").ap()
    if dump:
        DBG1 = nc.dram_tensor("DBG1", [128, 32896], BF16, kind="ExternalOutput").ap()
        DBG2 = nc.dram_tensor("DBG2", [128, 32768], BF16, kind="ExternalOutput").ap()
        DBG3 = nc.dram_tensor("DBG3", [128, 32768], BF16, kind="ExternalOutput").ap()
        DBG4 = nc.dram_tensor("DBG4", [128, 32768], BF16, kind="ExternalOutput").ap()
        DBG5 = nc.dram_tensor("DBG5", [128, 8192], BF16, kind="ExternalOutput").ap()
        DBG6 = nc.dram_tensor("DBG6", [64, 1024], BF16, kind="ExternalOutput").ap()
        DBG7 = nc.dram_tensor("DBG7", [16, 1024], F32, kind="ExternalOutput").ap()
        DBG8 = nc.dram_tensor("DBG8", [16, 1024], BF16, kind="ExternalOutput").ap()
        DBG9 = nc.dram_tensor("DBG9", [128, 32768], BF16, kind="ExternalOutput").ap()
        RD1 = nc.dram_tensor("RD1", [128, 128], F32, kind="ExternalOutput").ap()
        RD2 = nc.dram_tensor("RD2", [128, 128], F32, kind="ExternalOutput").ap()
        RD3 = nc.dram_tensor("RD3", [128, 64], F32, kind="ExternalOutput").ap()

    stack = ExitStack()
    with stack:
        E = stack.enter_context

        REG1 = E(nc.sbuf_tensor("REG1", [128, 32896], BF16))
        REG2 = E(nc.sbuf_tensor("REG2", [128, 32768], BF16))
        WST = E(nc.sbuf_tensor("WST", [128, 3 * 5120], BF16))
        IDB = E(nc.sbuf_tensor("IDB", [128, 128], BF16))
        IDF = E(nc.sbuf_tensor("IDF", [128, 128], F32))
        ONE4K = E(nc.sbuf_tensor("ONE4K", [128, 128], BF16))
        EPSB = E(nc.sbuf_tensor("EPSB", [128, 1], F32))

        KT = lambda g, a, b: REG1[:, 2048 * g + a: 2048 * g + b]
        VA = lambda kt, g: REG1[:, 16384 + (kt * 8 + g) * 129: 16384 + (kt * 8 + g) * 129 + 129]
        VAall = REG1[:, 16384:32896]

        def wslot(n, w=5120):
            i = n % 3
            return WST[:, 5120 * i: 5120 * i + w]

        p2 = ExitStack()
        MASKS = p2.enter_context(nc.sbuf_tensor("MASKS", [128, 4096], BF16))
        QT = p2.enter_context(nc.sbuf_tensor("QT", [128, 2 * 1024], BF16))
        UB = p2.enter_context(nc.sbuf_tensor("UB", [128, 2 * 1024], BF16))
        OTH = p2.enter_context(nc.sbuf_tensor("OTH", [128, 2 * 1024], BF16))
        OPR = p2.enter_context(nc.sbuf_tensor("OPR", [128, 2 * 256], BF16))
        RDN = p2.enter_context(nc.sbuf_tensor("RDN", [128, 4], F32))
        p1 = ExitStack()
        VTS = p1.enter_context(nc.sbuf_tensor("VTS", [128, 2 * 1024], BF16))

        PSA = E(nc.psum_tensor("PSA", [128, 2048], F32))
        PSB = E(nc.psum_tensor("PSB", [128, 1024], F32))
        PST1T = E(nc.psum_tensor("PST1", [128, 1024], BF16))
        PST2T = E(nc.psum_tensor("PST2", [128, 1024], BF16))
        PST1 = PST1T[:, 0:128]
        PST2 = PST2T[:, 0:128]

        ccon = Ctr(stack, nc, "ccon")
        csyc = Ctr(stack, nc, "csyc")
        cw = Ctr(stack, nc, "cw")
        cwr = Ctr(stack, nc, "cwr")
        cx = Ctr(stack, nc, "cx")
        cxr = Ctr(stack, nc, "cxr")
        ckv = Ctr(stack, nc, "ckv")
        ckve = Ctr(stack, nc, "ckve")
        ctr = Ctr(stack, nc, "ctr")
        ctre = Ctr(stack, nc, "ctre")
        cpk = Ctr(stack, nc, "cpk")
        cpkr = Ctr(stack, nc, "cpkr")
        cqf = Ctr(stack, nc, "cqf")
        cexp = Ctr(stack, nc, "cexp")
        cmsk = Ctr(stack, nc, "cmsk")
        cur = Ctr(stack, nc, "cur")
        cav = Ctr(stack, nc, "cav")
        cnrm = Ctr(stack, nc, "cnrm")
        cosp = Ctr(stack, nc, "cosp")
        cxf = Ctr(stack, nc, "cxf")
        cwo = Ctr(stack, nc, "cwo")
        chh = Ctr(stack, nc, "chh")
        cxo = Ctr(stack, nc, "cxo")
        cxor = Ctr(stack, nc, "cxor")
        csq = Ctr(stack, nc, "csq")
        cst = Ctr(stack, nc, "cst")
        cstm = Ctr(stack, nc, "cstm")
        cfin = Ctr(stack, nc, "cfin")
        cx1 = Ctr(stack, nc, "cx1")
        crld = Ctr(stack, nc, "crld")
        cart = Ctr(stack, nc, "cart")
        crt = Ctr(stack, nc, "crt")
        crte = Ctr(stack, nc, "crte")
        cct = Ctr(stack, nc, "cct")
        ccte = Ctr(stack, nc, "ccte")
        cffn = Ctr(stack, nc, "cffn")
        cffb = Ctr(stack, nc, "cffb")
        cpkq = Ctr(stack, nc, "cpkq")
        cx1g = Ctr(stack, nc, "cx1g")
        csqv = Ctr(stack, nc, "csqv")
        ch2 = Ctr(stack, nc, "ch2")
        cout = Ctr(stack, nc, "cout")
        coutg = Ctr(stack, nc, "coutg")

        blk = E(nc.Block())

        def emit_constants():
            @blk.gpsimd
            def _(g):
                ccon.inc(g.dma_start(MASKS[:, :], MSK), 16)
                ccon.inc(g.dma_start(IDB[:, :], IDD), 16)
                g.memset(VAall, 1.0)
                g.memset(EPSB[:, :], EPS)
                ccon.inc(g.memset(ONE4K[:, :], INV_D), 1)

        @blk.sync
        def _(s_):
            csyc.inc(s_.dma_start(IDF[:, :], IDD), 16)

        csyc_idf = csyc.v

        def xstage(cc):
            i = cc % 2
            return REG2[:, 16384 * i: 16384 * i + 16384]

        nblk = 0
        wslot_guard = []
        kv_evt = 0
        vts_free_at = [0, 0]
        pst_free_at = [0, 0]
        tr_idx = 0

        for cc in range(4):
            @blk.gpsimd
            def _(g, cc=cc):
                if cc >= 2:
                    g.wait_ge(ckv.h, 16 * (cc - 1))
                cx.inc(g.dma_start(xstage(cc), XT[cc]), 16)

            for gg in range(16):
                n = nblk
                nblk += 1

                @blk.gpsimd
                def _(g, n=n, gg=gg):
                    if n >= 3:
                        gd = wslot_guard[n - 3]
                        g.wait_ge(gd[0], gd[1])
                    cw.inc(g.dma_start(wslot(n, 4096), WKV[gg]), 16)

                if cc == 0 and gg == 0:
                    emit_constants()
                    ccon_ready = ccon.v

                ev = kv_evt
                kv_evt += 1
                r = ev % 4
                wslot_guard.append((ckv.h, ev + 1))

                @blk.tensor
                def _(t, n=n, ci=cc, ev=ev, r=r, gg=gg):
                    t.wait_ge(cw.h, 16 * (n + 1))
                    t.wait_ge(cx.h, 16 * (ci + 1))
                    if ev >= 4:
                        t.wait_ge(ckve.h, ev - 3)
                    for ct in range(32):
                        mm = t.matmul(
                            PSA[:, 512 * r: 512 * r + 512],
                            wslot(n, 4096)[:, 128 * ct: 128 * ct + 128],
                            xstage(cc)[:, 512 * ct: 512 * ct + 512],
                            start=(ct == 0), stop=(ct == 31),
                        )
                    ckv.inc(mm)

                if gg < 8:
                    @blk.vector
                    def _(v, ev=ev, r=r, g_=gg, cc=cc):
                        v.wait_ge(ckv.h, ev + 1)
                        ckve.inc(v.tensor_copy(
                            KT(g_, 512 * cc, 512 * cc + 512),
                            PSA[:, 512 * r: 512 * r + 512]))
                else:
                    vslot = ev % 2
                    need_tr = vts_free_at[vslot]

                    @blk.vector
                    def _(v, ev=ev, r=r, vslot=vslot, need_tr=need_tr):
                        v.wait_ge(ckv.h, ev + 1)
                        if need_tr:
                            v.wait_ge(ctr.h, need_tr)
                        ckve.inc(v.tensor_copy(
                            VTS[:, 1024 * vslot: 1024 * vslot + 512],
                            PSA[:, 512 * r: 512 * r + 512]))

                    my_evac = ckve.v
                    for tsub in range(4):
                        ti = tr_idx
                        tr_idx += 1
                        tp = ti % 2
                        PSTt = PST1 if tp == 0 else PST2
                        need_ev = pst_free_at[tp]

                        @blk.tensor
                        def _(t, tp=tp, PSTt=PSTt, vslot=vslot, ti=ti,
                              tsub=tsub, need_ev=need_ev, my_evac=my_evac):
                            t.wait_ge(ckve.h, my_evac)
                            if ti == 0:
                                t.wait_ge(ccon.h, 32)
                            if need_ev:
                                t.wait_ge(ctre.h, need_ev)
                            ctr.inc(t.transpose(
                                PSTt,
                                VTS[:, 1024 * vslot + 128 * tsub: 1024 * vslot + 128 * tsub + 128],
                                IDB[:, :]))

                        kt_g = 4 * cc + tsub
                        g_ = gg - 8

                        @blk.vector
                        def _(v, ti=ti, PSTt=PSTt, kt_g=kt_g, g_=g_):
                            v.wait_ge(ctr.h, ti + 1)
                            if ti == 0:
                                v.wait_ge(ccon.h, ccon_ready)
                            ctre.inc(v.tensor_copy(
                                VA(kt_g, g_)[:, 0:128], PSTt))

                        pst_free_at[tp] = ctre.v
                    vts_free_at[vslot] = ctr.v

        @blk.gpsimd
        def _(g):
            g.wait_ge(ckv.h, 48)
            cx.inc(g.dma_start(REG2[:, 0:16384], XOWN[:, 0:16, :]), 16)
        xown_r0 = cx.v

        @blk.gpsimd
        def _(g):
            g.wait_ge(ckv.h, ckv.v)
            cx.inc(g.dma_start(REG2[:, 16384:24576], XOWN[:, 16:24, :]), 16)
        xown_r1 = cx.v

        @blk.gpsimd
        def _(g):
            cx.inc(g.dma_start(REG2[:, 24576:32768], XOWN[:, 24:32, :]), 16)
        xown_r2 = cx.v

        p1.close()
        if phases < 2:
            p2.close()
            return nc

        pk = 0
        u_idx = 0
        av_idx = 0
        m_idx = 0
        kvdone = ckve.v
        va_done = ctre.v
        if dump:
            @blk.sync
            def _(s_):
                s_.wait_ge(ckve.h, kvdone)
                s_.wait_ge(ctre.h, va_done)
                cosp.inc(s_.dma_start(DBG1, REG1[:, :]), 16)

        q31_fill = [0]
        head_lastS = {}
        pack_rel = []

        def pack_off(p):
            return 1024 * (p % 2)

        qt_slice = lambda h, a, b: QT[:, 1024 * (h % 2) + a: 1024 * (h % 2) + b]

        def emit_qproj(h):
            nonlocal pk, nblk
            n = nblk
            nblk += 1

            @blk.gpsimd
            def _(g, n=n, h=h):
                gd = wslot_guard[n - 3]
                g.wait_ge(gd[0], gd[1])
                cw.inc(g.dma_start(wslot(n, 4096), WQ[h]), 16)

            mypk = pk
            pk += 1
            off = pack_off(mypk)
            wslot_guard.append((cpk.h, mypk + 1))

            @blk.tensor
            def _(t, n=n, mypk=mypk, off=off, h=h):
                t.wait_ge(cw.h, 16 * (n + 1))
                if mypk >= 2:
                    gd = pack_rel[mypk - 2]
                    t.wait_ge(gd[0], gd[1])
                for half in range(2):
                    for ct in range(32):
                        if h == 0 and half == 0:
                            if ct == 0:
                                t.wait_ge(cx.h, xown_r0)
                            elif ct == 16:
                                t.wait_ge(cx.h, xown_r1)
                            elif ct == 24:
                                t.wait_ge(cx.h, xown_r2)
                        mm = t.matmul(
                            PSA[:, off + 512 * half: off + 512 * half + 512],
                            wslot(n, 4096)[:, 128 * ct: 128 * ct + 128],
                            REG2[:, 1024 * ct + 512 * half: 1024 * ct + 512 * half + 512],
                            start=(ct == 0), stop=(ct == 31),
                        )
                cpk.inc(mm)
                if h == 31:
                    q31_fill[0] = cpk.v

            @blk.vector
            def _(v, mypk=mypk, off=off, h=h):
                v.wait_ge(cpk.h, mypk + 1)
                if h >= 2:
                    v.wait_ge(cpk.h, head_lastS[h - 2])
                v.tensor_copy(qt_slice(h, 0, 512), PSA[:, off: off + 512])
                cpkq.inc(v.tensor_copy(qt_slice(h, 512, 1024), PSA[:, off + 512: off + 1024]))
            pack_rel.append((cpkq.h, cpkq.v))

        from collections import deque
        workq = deque()
        last_oth = []
        exp_jobs = []
        tr_base = ctr.v
        tr_created = 0
        av_created = 0
        round_tr_last = {}
        pst_guard = [0, 0]
        ctre_assign = [ctre.v]

        def flush(n_keep=2):
            while len(workq) > n_keep:
                workq.popleft()()

        def emit_attn(h):
            nonlocal pk, u_idx, av_idx, m_idx, tr_created, av_created
            g_ = h // 4
            for s in range(4):
                myav = av_idx
                av_idx += 1
                a_off = 512 * (myav % 2)
                npacks = s + 1
                for p in range(npacks):
                    mypk = pk
                    pk += 1
                    myu = u_idx
                    u_idx += 1
                    off = pack_off(mypk)
                    masked = (p == s)

                    @blk.tensor
                    def _(t, mypk=mypk, off=off, s=s, p=p, h=h, g_=g_):
                        if mypk >= 2:
                            gd = pack_rel[mypk - 2]
                            t.wait_ge(gd[0], gd[1])
                        if h == 0 and s == 0 and p == 0:
                            t.wait_ge(ckve.h, kvdone)
                        for j in range(4):
                            kt = 4 * p + j
                            mm = t.matmul(
                                PSA[:, off + 256 * j: off + 256 * j + 256],
                                KT(g_, 128 * kt, 128 * kt + 128),
                                qt_slice(h, 256 * s, 256 * s + 256),
                                start=True, stop=True,
                            )
                        cpk.inc(mm)

                    exp_jobs.append((mypk, myu, off))
                    pack_rel.append((cpkr.h, len(exp_jobs)))
                    gd_pack = pack_rel[-1]

                    if masked:
                        m_idx += 1
                        mym = m_idx

                        @blk.vector
                        def _(v, gd=gd_pack, myu=myu, s=s):
                            v.wait_ge(gd[0], gd[1])
                            u = UB[:, 1024 * (myu % 2): 1024 * (myu % 2) + 1024]
                            cmsk.inc(v.tensor_tensor(
                                u, u, MASKS[:, 1024 * s: 1024 * s + 1024],
                                op=AluOpType.mult))
                    else:
                        mym = m_idx

                    av_created += 1
                    my_cur_val = av_created
                    first_av = (p == 0)

                    def av_item(myu=myu, masked=masked, mym=mym, p=p, s=s,
                                g_=g_, a_off=a_off, first_av=first_av,
                                myav=myav, gd=gd_pack):
                        @blk.tensor
                        def _(t):
                            if masked:
                                t.wait_ge(cmsk.h, mym)
                            else:
                                t.wait_ge(gd[0], gd[1])
                            if first_av:
                                if myav == 0:
                                    t.wait_ge(ctre.h, va_done)
                                if myav >= 2:
                                    t.wait_ge(cnrm.h, myav - 1)
                            u0 = 1024 * (myu % 2)
                            first = first_av
                            for j in range(4):
                                kt = 4 * p + j
                                for qs in range(2):
                                    mm = t.matmul(
                                        PSB[:, a_off + 129 * qs: a_off + 129 * qs + 129],
                                        UB[:, u0 + 256 * j + 128 * qs: u0 + 256 * j + 128 * qs + 128],
                                        VA(kt, g_),
                                        start=first, stop=(p == s and j == 3 and qs == 1),
                                    )
                                    first = False
                            cur.inc(mm)
                    workq.append(av_item)
                    flush()

                opr_gd = round_tr_last.get(myav - 2, 0)

                @blk.vector
                def _(v, myav=myav, a_off=a_off, avval=my_cur_val, opr_gd=opr_gd):
                    v.wait_ge(cur.h, avval)
                    if opr_gd:
                        v.wait_ge(ctr.h, opr_gd)
                    o = 256 * (myav % 2)
                    v.reciprocal(RDN[:, 0:1], PSB[:, a_off + 128: a_off + 129])
                    v.reciprocal(RDN[:, 1:2], PSB[:, a_off + 257: a_off + 258])
                    v.drain()
                    v.tensor_scalar(OPR[:, o: o + 128], PSB[:, a_off: a_off + 128],
                                    RDN[:, 0:1], None, op0=AluOpType.mult)
                    cnrm.inc(v.tensor_scalar(
                        OPR[:, o + 128: o + 256], PSB[:, a_off + 129: a_off + 257],
                        RDN[:, 1:2], None, op0=AluOpType.mult))
                mynrm = cnrm.v

                for qs in range(2):
                    ti = tr_created
                    tr_created += 1
                    tp = ti % 2
                    PSTt = PST1 if tp == 0 else PST2
                    need_ev = pst_guard[tp]
                    ctre_assign[0] += 1
                    my_ctre = ctre_assign[0]
                    pst_guard[tp] = my_ctre

                    def tr_item(myav=myav, qs=qs, PSTt=PSTt, need_ev=need_ev,
                                mynrm=mynrm, ti=ti, h=h, s=s):
                        @blk.tensor
                        def _(t):
                            t.wait_ge(cnrm.h, mynrm)
                            if need_ev:
                                t.wait_ge(ctre.h, need_ev)
                            o = 256 * (myav % 2)
                            ctr.inc(t.transpose(
                                PSTt, OPR[:, o + 128 * qs: o + 128 * qs + 128],
                                IDB[:, :]))

                        @blk.vector
                        def _(v):
                            v.wait_ge(ctr.h, tr_base + ti + 1)
                            if h >= 2 and s == 0 and qs == 0:
                                v.wait_ge(cosp.h, 16 * (h - 1))
                            ctre.inc(v.tensor_copy(
                                OTH[:, 1024 * (h % 2) + 256 * s + 128 * qs:
                                     1024 * (h % 2) + 256 * s + 128 * qs + 128],
                                PSTt))
                    workq.append(tr_item)
                    flush()
                round_tr_last[myav] = tr_base + tr_created

            head_lastS[h] = cpk.v

            myctre = ctre_assign[0]

            if h < nheads - 2:
                @blk.sync
                def _(s_, h=h, myctre=myctre):
                    s_.wait_ge(ctre.h, myctre)
                    cosp.inc(s_.dma_start(
                        OSC[:, 1024 * h: 1024 * h + 1024],
                        OTH[:, 1024 * (h % 2): 1024 * (h % 2) + 1024]), 16)
            else:
                last_oth.append((h, myctre))

        emit_qproj(0)
        emit_qproj(1)
        for h in range(nheads):
            emit_attn(h)
            if h + 2 < nheads:
                emit_qproj(h + 2)
        flush(0)

        @blk.scalar
        def _(sc):
            for i, (mypk, myu, off) in enumerate(exp_jobs):
                sc.wait_ge(cpk.h, mypk + 1)
                if myu >= 2:
                    sc.wait_ge(cur.h, myu - 1)
                act = sc.activation(
                    UB[:, 1024 * (myu % 2): 1024 * (myu % 2) + 1024],
                    PSA[:, off: off + 1024],
                    AF.Exp, scale=SCALE)
                act.then_inc(cpkr.h, 1)
        cpkr.v += len(exp_jobs)

        if nheads == 32:
            @blk.sync
            def _(s_):
                s_.wait_ge(cosp.h, cosp.v)
                s_.wait_ge(cpk.h, q31_fill[0])
                cosp.inc(s_.dma_start(REG2[:, 0:30720], OSC[:, 0:30720]), 16)

            for h, myctre in last_oth:
                @blk.vector
                def _(v, h=h, myctre=myctre):
                    v.wait_ge(ctre.h, myctre)
                    v.wait_ge(cpk.h, q31_fill[0])
                    crld.inc(v.tensor_copy(
                        REG2[:, 1024 * h: 1024 * h + 1024],
                        OTH[:, 1024 * (h % 2): 1024 * (h % 2) + 1024]))
        ot_ready = cosp.v
        oth_copied = crld.v
        if dump and nheads == 32:
            @blk.sync
            def _(s_):
                s_.wait_ge(cosp.h, ot_ready)
                cosp.inc(s_.dma_start(DBG2, REG2[:, :]), 16)
        ot_ready = cosp.v

        p2.close()
        if phases < 3:
            return nc

        p3 = ExitStack()
        SQ = p3.enter_context(nc.sbuf_tensor("SQ", [128, 2 * 512], BF16))
        SD = p3.enter_context(nc.sbuf_tensor("SD", [128, 512], F32))
        M2SB = p3.enter_context(nc.sbuf_tensor("M2SB", [128, 512], F32))
        RSTD2 = p3.enter_context(nc.sbuf_tensor("RSTD2", [128, 1024], BF16))
        BT2 = p3.enter_context(nc.sbuf_tensor("BT2", [128, 1024], BF16))
        p3a = ExitStack()
        XO = p3a.enter_context(nc.sbuf_tensor("XO", [128, 3 * 1024], F32))

        st_sum = [0, 1024]
        st_sq = [512, 1536]

        def emit_ln_finalize(tc, mystats, wide=True):
            sum_off, sq_off = st_sum[tc], st_sq[tc]
            RS_ = RSTD2[:, 512 * tc: 512 * tc + 512]
            BT_ = BT2[:, 512 * tc: 512 * tc + 512]

            @blk.vector
            def _(v, sum_off=sum_off, sq_off=sq_off, mystats=mystats):
                v.wait_ge(cstm.h, mystats)
                v.tensor_copy(M2SB[:, :], PSA[:, sum_off: sum_off + 512])
                v.drain()
                v.tensor_tensor(M2SB[:, :], M2SB[:, :], M2SB[:, :], op=AluOpType.mult)
                csq.inc(v.tensor_tensor(M2SB[:, :], PSA[:, sq_off: sq_off + 512],
                                        M2SB[:, :], op=AluOpType.subtract))
            myv = csq.v

            @blk.scalar
            def _(sc, myv=myv):
                sc.wait_ge(csq.h, myv)
                csq.inc(sc.activation(SD[:, :], M2SB[:, :], AF.Sqrt, bias=EPSB[:, 0:1]))
            mysd = csq.v

            @blk.vector
            def _(v, sum_off=sum_off, mysd=mysd, RS_=RS_, BT_=BT_):
                v.wait_ge(csq.h, mysd)
                with nc.allow_low_precision(reason="bf16 rstd, validated margin"):
                    v.reciprocal(RS_, SD[:, :])
                    v.drain()
                    cfin.inc(v.tensor_tensor(BT_, PSA[:, sum_off: sum_off + 512],
                                             RS_, op=AluOpType.mult))

        wo_rounds = 0
        pending_stats = []

        def flush_stats(keep=1):
            while len(pending_stats) > keep:
                pending_stats.pop(0)()

        for ot in range(32):
            n = nblk
            nblk += 1
            wslot_guard.append((cwo.h, 2 * (ot + 1)))

            @blk.gpsimd
            def _(g, n=n, ot=ot):
                gd = wslot_guard[n - 3]
                g.wait_ge(gd[0], gd[1])
                cw.inc(g.dma_start(wslot(n, 4096), WOD[ot]), 16)

            @blk.sync
            def _(s_, ot=ot):
                if ot >= 3:
                    s_.wait_ge(chh.h, 2 * ot - 4)
                elif ot == 0:
                    s_.wait_ge(crld.h, oth_copied)
                cxo.inc(s_.dma_start(XO[:, 1024 * (ot % 3): 1024 * (ot % 3) + 1024],
                                     XOWN[:, ot, :]), 16)

            for tc in range(2):
                wo_rounds += 1
                rnd = wo_rounds
                rr = (rnd - 1) % 2
                b_off = 512 * rr

                @blk.tensor
                def _(t, n=n, ot=ot, tc=tc, b_off=b_off, rnd=rnd):
                    t.wait_ge(cw.h, 16 * (n + 1))
                    if ot == 0 and tc == 0:
                        t.wait_ge(cosp.h, ot_ready)
                        t.wait_ge(crld.h, oth_copied)
                    if rnd > 2:
                        t.wait_ge(chh.h, rnd - 2)
                    for ht in range(32):
                        mm = t.matmul(
                            PSB[:, b_off: b_off + 512],
                            wslot(n, 4096)[:, 128 * ht: 128 * ht + 128],
                            REG2[:, 1024 * ht + 512 * tc: 1024 * ht + 512 * tc + 512],
                            start=(ht == 0), stop=(ht == 31),
                        )
                    cwo.inc(mm)
                flush_stats()

                mywo = cwo.v

                @blk.vector
                def _(v, ot=ot, tc=tc, b_off=b_off, mywo=mywo):
                    v.wait_ge(cwo.h, mywo)
                    v.wait_ge(cxo.h, 16 * (ot + 1))
                    mm = v.tensor_tensor(
                        REG1[:, 1024 * ot + 512 * tc: 1024 * ot + 512 * tc + 512],
                        PSB[:, b_off: b_off + 512],
                        XO[:, 1024 * (ot % 3) + 512 * tc: 1024 * (ot % 3) + 512 * tc + 512],
                        op=AluOpType.add)
                    chh.inc(mm)

                myhh = chh.v

                @blk.vector
                def _(v, ot=ot, tc=tc, rnd=rnd, rr=rr):
                    if rnd > 2:
                        v.wait_ge(cstm.h, rnd - 2)
                    sl = REG1[:, 1024 * ot + 512 * tc: 1024 * ot + 512 * tc + 512]
                    csqv.inc(v.tensor_tensor(
                        SQ[:, 512 * rr: 512 * rr + 512], sl, sl, op=AluOpType.mult))

                mysq = csqv.v

                def stats_item(ot=ot, tc=tc, mysq=mysq, rr=rr):
                    @blk.tensor
                    def _(t):
                        t.wait_ge(csqv.h, mysq)
                        t.matmul(PSA[:, st_sum[tc]: st_sum[tc] + 512], ONE4K[:, :],
                                 REG1[:, 1024 * ot + 512 * tc: 1024 * ot + 512 * tc + 512],
                                 start=(ot == 0), stop=(ot == 31))
                        mm = t.matmul(PSA[:, st_sq[tc]: st_sq[tc] + 512], ONE4K[:, :],
                                      SQ[:, 512 * rr: 512 * rr + 512],
                                      start=(ot == 0), stop=(ot == 31))
                        cstm.inc(mm)
                pending_stats.append(stats_item)
        flush_stats(0)

        emit_ln_finalize(0, cstm.v)
        fin0 = cfin.v
        emit_ln_finalize(1, cstm.v)
        fin1 = cfin.v
        x1v_base = cx1.v
        for tc in range(2):
            myfin = fin0 if tc == 0 else fin1
            for ot in range(32):
                @blk.vector
                def _(v, ot=ot, tc=tc, myfin=myfin):
                    v.wait_ge(cfin.h, myfin)
                    v.tensor_tensor(SQ[:, 512 * tc: 512 * tc + 512],
                                    REG1[:, 1024 * ot + 512 * tc: 1024 * ot + 512 * tc + 512],
                                    RSTD2[:, 512 * tc: 512 * tc + 512],
                                    op=AluOpType.mult)
                    cx1.inc(v.tensor_tensor(
                        REG2[:, 1024 * ot + 512 * tc: 1024 * ot + 512 * tc + 512],
                        SQ[:, 512 * tc: 512 * tc + 512], BT2[:, 512 * tc: 512 * tc + 512],
                        op=AluOpType.subtract))
        x1_done = cx1.v
        if dump:
            @blk.sync
            def _(s_):
                s_.wait_ge(cx1.h, x1_done)
                cosp.inc(s_.dma_start(DBG3, REG2[:, 0:32768]), 16)
                cosp.inc(s_.dma_start(DBG4, REG1[:, 0:32768]), 16)
        if phases < 4:
            p3a.close()
            p3.close()
            return nc

        p3a.close()
        p4 = ExitStack()
        CT = p4.enter_context(nc.sbuf_tensor("CT", [128, 8192], BF16))
        RH = p4.enter_context(nc.sbuf_tensor("RH", [64, 1024], BF16))
        LGT = p4.enter_context(nc.sbuf_tensor("LGT", [16, 1024], F32))
        LGtm = p4.enter_context(nc.sbuf_tensor("LGtm", [128, 128], F32))
        CMB = p4.enter_context(nc.sbuf_tensor("CMB", [16, 1024], BF16))
        M1S = p4.enter_context(nc.sbuf_tensor("M1S", [16, 1024], BF16))
        M2S = p4.enter_context(nc.sbuf_tensor("M2S", [64, 128], BF16))
        BBS = p4.enter_context(nc.sbuf_tensor("BBS", [128, 32], F32))
        ETMP = p4.enter_context(nc.sbuf_tensor("ETMP", [128, 16], F32))
        PTMP = p4.enter_context(nc.sbuf_tensor("PTMP", [128, 16], F32))
        GTMP = p4.enter_context(nc.sbuf_tensor("GTMP", [128, 24], F32))
        SCL = p4.enter_context(nc.sbuf_tensor("SCL", [128, 8], F32))
        ETM8 = p4.enter_context(nc.sbuf_tensor("ETM8", [128, 128], F32))
        CMBtm = LGtm
        RDB1 = p4.enter_context(nc.sbuf_tensor("RDB1", [128, 128], F32))
        RDB2 = p4.enter_context(nc.sbuf_tensor("RDB2", [128, 128], F32))
        RDB3 = p4.enter_context(nc.sbuf_tensor("RDB3", [128, 64], F32))
        OST = p4.enter_context(nc.sbuf_tensor("OST", [128, 4 * 512], F32))

        @blk.sync
        def _(s_):
            s_.wait_ge(chh.h, 64)
            csyc.inc(s_.dma_start(BBS[:, :], BBD), 16)
        bb_ready = csyc.v

        n_art = nblk
        nblk += 1
        wslot_guard.append((cart.h, 1))

        @blk.gpsimd
        def _(g, n=n_art):
            gd = wslot_guard[n - 3]
            g.wait_ge(gd[0], gd[1])
            cw.inc(g.dma_start(wslot(n, 2560), ARTW), 16)

        n_ffb0 = nblk
        nblk += 1
        wslot_guard.append((cffn.h, 1))

        @blk.gpsimd
        def _(g, n=n_ffb0):
            gd = wslot_guard[n - 3]
            g.wait_ge(gd[0], gd[1])
            cw.inc(g.dma_start(wslot(n, 5120), FFB[0]), 16)

        n_ffb1 = nblk
        nblk += 1
        wslot_guard.append((cffn.h, 2))

        @blk.gpsimd
        def _(g, n=n_ffb1):
            gd = wslot_guard[n - 3]
            g.wait_ge(gd[0], gd[1])
            cw.inc(g.dma_start(wslot(n, 5120), FFB[1]), 16)

        @blk.gpsimd
        def _(g):
            g.wait_ge(chh.h, 64)
            ccon.inc(g.dma_start(M1S[:, :], M1D), 16)
            ccon.inc(g.dma_start(M2S[:, :], M2D), 16)
        m_ready = ccon.v

        @blk.tensor
        def _(t, n=n_ffb0):
            t.wait_ge(cw.h, 16 * (n + 1))
            for ct in range(32):
                t.wait_ge(cx1.h, x1v_base + ct + 1)
                t.matmul(PSB[:, 0:512],
                         wslot(n, 5120)[:, 128 * ct: 128 * ct + 128],
                         REG2[:, 1024 * ct: 1024 * ct + 512],
                         start=(ct == 0), stop=False)

        @blk.tensor
        def _(t, n=n_ffb1):
            t.wait_ge(cw.h, 16 * (n + 1))
            for ct in range(32):
                t.matmul(PSB[:, 512:1024],
                         wslot(n, 5120)[:, 128 * ct: 128 * ct + 128],
                         REG2[:, 1024 * ct: 1024 * ct + 512],
                         start=(ct == 0), stop=False)

        @blk.tensor
        def _(t, n=n_art, x1_done=x1_done):
            t.wait_ge(cw.h, 16 * (n + 1))
            t.wait_ge(cx1.h, x1_done)
            for half in range(2):
                for ct in range(32):
                    mm = t.matmul(
                        PSA[0:80, 512 * half: 512 * half + 512],
                        wslot(n, 2560)[:, 80 * ct: 80 * ct + 80],
                        REG2[:, 1024 * ct + 512 * half: 1024 * ct + 512 * half + 512],
                        start=(ct == 0), stop=(ct == 31),
                    )
            cart.inc(mm)
        art_mm = cart.v

        @blk.scalar
        def _(sc, art_mm=art_mm):
            sc.wait_ge(cart.h, art_mm)
            sc.copy(RH[:, :], PSA[0:64, 0:1024])
            cart.inc(sc.copy(LGT[:, :], PSA[64:80, 0:1024]))
        art_ev = cart.v

        @blk.tensor
        def _(t, art_ev=art_ev):
            t.wait_ge(cart.h, art_ev)
            t.wait_ge(csyc.h, csyc_idf)
            for tt in range(8):
                mm = t.matmul(PSA[:, 1024 + 16 * tt: 1024 + 16 * tt + 16],
                              LGT[:, 128 * tt: 128 * tt + 128],
                              IDF[0:16, 0:16], start=True, stop=True)
            crt.inc(mm)
        lt_done = crt.v

        @blk.vector
        def _(v, lt_done=lt_done):
            v.wait_ge(crt.h, lt_done)
            v.tensor_copy(LGtm[:, 0:128], PSA[:, 1024:1152])
            v.drain()
            for tt in range(8):
                v.reduce_max(SCL[:, tt: tt + 1], LGtm[:, 16 * tt: 16 * tt + 16], axis=AX.X)
                v.drain()
            crte.inc(v.tensor_scalar(GTMP[:, 0:8], SCL[:, 0:8], -1.0, None,
                                     op0=AluOpType.mult))
        negmax_done = crte.v

        @blk.scalar
        def _(sc, negmax_done=negmax_done):
            sc.wait_ge(crte.h, negmax_done)
            for tt in range(8):
                act = sc.activation(ETM8[:, 16 * tt: 16 * tt + 16],
                                    LGtm[:, 16 * tt: 16 * tt + 16],
                                    AF.Exp, bias=GTMP[:, tt: tt + 1])
            csq.inc(act)
        exp8_done = csq.v

        @blk.vector
        def _(v, exp8_done=exp8_done):
            v.wait_ge(csq.h, exp8_done)
            for tt in range(8):
                E8 = ETM8[:, 16 * tt: 16 * tt + 16]
                v.drain()
                v.reduce_sum(SCL[:, 2:3], E8, axis=AX.X)
                v.drain()
                v.reciprocal(SCL[:, 3:4], SCL[:, 2:3])
                v.drain()
                v.tensor_scalar(PTMP[:, :], E8, SCL[:, 3:4], None, op0=AluOpType.mult)
                v.drain()
                v.reduce_max(SCL[:, 4:5], PTMP[:, :], axis=AX.X)
                v.drain()
                v.tensor_scalar(GTMP[:, 8:24], PTMP[:, :], SCL[:, 4:5], None, op0=AluOpType.is_ge)
                v.drain()
                v.tensor_tensor(ETMP[:, :], PTMP[:, :], GTMP[:, 8:24], op=AluOpType.subtract)
                v.drain()
                v.reduce_max(SCL[:, 5:6], ETMP[:, :], axis=AX.X)
                v.drain()
                v.tensor_tensor(SCL[:, 6:7], SCL[:, 4:5], SCL[:, 5:6], op=AluOpType.add)
                v.drain()
                v.tensor_scalar(SCL[:, 6:7], SCL[:, 6:7], 1e-6, None, op0=AluOpType.add)
                v.drain()
                v.reciprocal(SCL[:, 7:8], SCL[:, 6:7])
                v.drain()
                v.tensor_scalar(SCL[:, 7:8], SCL[:, 7:8], 0.5, None, op0=AluOpType.mult)
                v.drain()
                v.tensor_scalar(GTMP[:, 8:24], PTMP[:, :], SCL[:, 5:6], None, op0=AluOpType.is_ge)
                v.drain()
                v.tensor_tensor(PTMP[:, :], PTMP[:, :], GTMP[:, 8:24], op=AluOpType.mult)
                v.drain()
                mm = v.tensor_scalar(CMBtm[:, 16 * tt: 16 * tt + 16], PTMP[:, :],
                                     SCL[:, 7:8], None, op0=AluOpType.mult)
            crte.inc(mm)
        cmb_done = crte.v

        @blk.tensor
        def _(t, cmb_done=cmb_done):
            t.wait_ge(crte.h, cmb_done)
            for tt in range(8):
                mm = t.matmul(PSA[0:16, 128 * tt: 128 * tt + 128],
                              CMBtm[:, 16 * tt: 16 * tt + 16],
                              IDF[:, :], start=True, stop=True)
            crt.inc(mm)
        cmbT_done = crt.v

        @blk.vector
        def _(v, cmbT_done=cmbT_done):
            v.wait_ge(crt.h, cmbT_done)
            crte.inc(v.tensor_copy(CMB[:, :], PSA[0:16, 0:1024]))
        router_done = crte.v

        for j in range(8):
            @blk.tensor
            def _(t, j=j, router_done=router_done):
                t.wait_ge(crte.h, router_done)
                t.wait_ge(ccon.h, m_ready)
                if j >= 1:
                    t.wait_ge(ccte.h, j)
                for half in range(2):
                    t.matmul(PSA[:, 1024 + 512 * half: 1024 + 512 * half + 512],
                             M1S[:, 128 * j: 128 * j + 128],
                             CMB[:, 512 * half: 512 * half + 512],
                             start=True, stop=True)
                    mm = t.matmul(PSA[:, 512 * half: 512 * half + 512],
                                  M2S[:, :],
                                  RH[:, 512 * half: 512 * half + 512],
                                  start=True, stop=True)
                cct.inc(mm)
            myj = cct.v

            @blk.vector
            def _(v, j=j, myj=myj):
                v.wait_ge(cct.h, myj)
                for half in range(2):
                    v.tensor_copy(M2SB[:, :], PSA[:, 1024 + 512 * half: 1024 + 512 * half + 512])
                    mm = v.tensor_tensor(
                        CT[:, 1024 * j + 512 * half: 1024 * j + 512 * half + 512],
                        M2SB[:, :], PSA[:, 512 * half: 512 * half + 512],
                        op=AluOpType.mult)
                ccte.inc(mm)
        ct_done = ccte.v
        if dump:
            @blk.sync
            def _(s_):
                s_.wait_ge(ccte.h, ct_done)
                cosp.inc(s_.dma_start(DBG5, CT[:, :]), 16)
                cosp.inc(s_.dma_start(DBG6, RH[:, :]), 16)
                cosp.inc(s_.dma_start(DBG7, LGT[:, :]), 16)
                cosp.inc(s_.dma_start(DBG8, CMB[:, :]), 16)

        ct_base = ct_done - 8
        for rnd12, nslot in ((1, n_ffb0), (2, n_ffb1)):
            @blk.tensor
            def _(t, n=nslot, b_off=512 * (rnd12 - 1), ct_base=ct_base):
                for j in range(8):
                    t.wait_ge(ccte.h, ct_base + j + 1)
                    mm = t.matmul(PSB[:, b_off: b_off + 512],
                                  wslot(n, 5120)[:, 4096 + 128 * j: 4096 + 128 * j + 128],
                                  CT[:, 1024 * j: 1024 * j + 512],
                                  start=False, stop=(j == 7))
                cffn.inc(mm)

        ffn_rounds = 0

        def emit_ffn_epilogue(ot, tc, rnd):
            rr = (rnd - 1) % 2

            myffn = cffn.v

            @blk.vector
            def _(v, ot=ot, tc=tc, b_off=512 * rr, myffn=myffn):
                v.wait_ge(cffn.h, myffn)
                v.wait_ge(csyc.h, bb_ready)
                mm = v.scalar_tensor_tensor(
                    REG1[:, 1024 * ot + 512 * tc: 1024 * ot + 512 * tc + 512],
                    PSB[:, b_off: b_off + 512],
                    BBS[:, ot: ot + 1],
                    REG2[:, 1024 * ot + 512 * tc: 1024 * ot + 512 * tc + 512],
                    op0=AluOpType.add, op1=AluOpType.add)
                ch2.inc(mm)

            @blk.vector
            def _(v, ot=ot, tc=tc, rnd=rnd, rr=rr):
                if rnd > 2:
                    v.wait_ge(cstm.h, 64 + rnd - 2)
                sl = REG1[:, 1024 * ot + 512 * tc: 1024 * ot + 512 * tc + 512]
                csqv.inc(v.tensor_tensor(
                    SQ[:, 512 * rr: 512 * rr + 512], sl, sl, op=AluOpType.mult))

            mysq = csqv.v

            def stats_item(ot=ot, tc=tc, mysq=mysq, rr=rr):
                @blk.tensor
                def _(t):
                    t.wait_ge(csqv.h, mysq)
                    t.matmul(PSA[:, st_sum[tc]: st_sum[tc] + 512], ONE4K[:, :],
                             REG1[:, 1024 * ot + 512 * tc: 1024 * ot + 512 * tc + 512],
                             start=(ot == 0), stop=(ot == 31))
                    mm = t.matmul(PSA[:, st_sq[tc]: st_sq[tc] + 512], ONE4K[:, :],
                                  SQ[:, 512 * rr: 512 * rr + 512],
                                  start=(ot == 0), stop=(ot == 31))
                    cstm.inc(mm)
            pending_stats.append(stats_item)

        out_n = [0]

        def emit_out(ot, tc, myfin):
            out_n[0] += 1
            n = out_n[0]
            oslot = (n - 1) % 4
            qc = cout if (n - 1) % 2 == 0 else coutg

            @blk.vector
            def _(v, ot=ot, tc=tc, myfin=myfin, oslot=oslot, n=n, qc=qc):
                v.wait_ge(cfin.h, myfin)
                if n > 4:
                    v.wait_ge(qc.h, 16 * ((n + 1) // 2 - 2))
                osl = OST[:, 512 * oslot: 512 * oslot + 512]
                v.tensor_tensor(osl,
                                REG1[:, 1024 * ot + 512 * tc: 1024 * ot + 512 * tc + 512],
                                RSTD2[:, 512 * tc: 512 * tc + 512], op=AluOpType.mult)
                cx1.inc(v.tensor_tensor(
                    osl, osl, BT2[:, 512 * tc: 512 * tc + 512], op=AluOpType.subtract))

            myo = cx1.v

            if (n - 1) % 2 == 0:
                @blk.sync
                def _(s_, ot=ot, tc=tc, oslot=oslot, myo=myo):
                    s_.wait_ge(cx1.h, myo)
                    cout.inc(s_.dma_start(
                        OUT[128 * ot: 128 * ot + 128, 512 * tc: 512 * tc + 512],
                        OST[:, 512 * oslot: 512 * oslot + 512]), 16)
            else:
                @blk.gpsimd
                def _(g, ot=ot, tc=tc, oslot=oslot, myo=myo):
                    g.wait_ge(cx1.h, myo)
                    coutg.inc(g.dma_start(
                        OUT[128 * ot: 128 * ot + 128, 512 * tc: 512 * tc + 512],
                        OST[:, 512 * oslot: 512 * oslot + 512]), 16)

        for rnd12 in (1, 2):
            ffn_rounds += 1
            emit_ffn_epilogue(rnd12 - 1, 0, rnd12)

        norm_q = []
        for sweep_tc in range(2):
            for ot in (range(2, 32) if sweep_tc == 0 else range(32)):
                n = nblk
                nblk += 1
                ffn_rounds += 1
                rnd = ffn_rounds
                wslot_guard.append((cffn.h, rnd))

                @blk.gpsimd
                def _(g, n=n, ot=ot):
                    gd = wslot_guard[n - 3]
                    g.wait_ge(gd[0], gd[1])
                    cw.inc(g.dma_start(wslot(n, 5120), FFB[ot]), 16)

                b_off = 512 * ((rnd - 1) % 2)

                @blk.tensor
                def _(t, n=n, tc=sweep_tc, b_off=b_off, rnd=rnd):
                    t.wait_ge(cw.h, 16 * (n + 1))
                    t.wait_ge(ch2.h, rnd - 2)
                    for ct in range(32):
                        t.matmul(PSB[:, b_off: b_off + 512],
                                 wslot(n, 5120)[:, 128 * ct: 128 * ct + 128],
                                 REG2[:, 1024 * ct + 512 * tc: 1024 * ct + 512 * tc + 512],
                                 start=(ct == 0), stop=False)
                    for j in range(8):
                        mm = t.matmul(PSB[:, b_off: b_off + 512],
                                      wslot(n, 5120)[:, 4096 + 128 * j: 4096 + 128 * j + 128],
                                      CT[:, 1024 * j + 512 * tc: 1024 * j + 512 * tc + 512],
                                      start=False, stop=(j == 7))
                    cffn.inc(mm)
                flush_stats()
                emit_ffn_epilogue(ot, sweep_tc, rnd)
                for _i in range(2):
                    if norm_q:
                        norm_q.pop(0)()

            flush_stats(0)
            emit_ln_finalize(sweep_tc, cstm.v)
            myfin = cfin.v
            for oto in range(32):
                norm_q.append(lambda oto=oto, tc=sweep_tc, myfin=myfin:
                              emit_out(oto, tc, myfin))

        while norm_q:
            norm_q.pop(0)()

        if dump:
            @blk.sync
            def _(s_):
                s_.wait_ge(ch2.h, ch2.v)
                cosp.inc(s_.dma_start(DBG9, REG1[:, 0:32768]), 16)
                cosp.inc(s_.dma_start(RD1, RDB1[:, :]), 16)
                cosp.inc(s_.dma_start(RD2, RDB2[:, :]), 16)
                cosp.inc(s_.dma_start(RD3, RDB3[:, :]), 16)

        @blk.sync
        def _(s_):
            s_.wait_ge(cout.h, cout.v)
            s_.wait_ge(coutg.h, coutg.v)

        p4.close()
        p3.close()

    return nc


def _host_prep(inputs):
    f32 = np.float32
    x = np.ascontiguousarray(np.asarray(inputs["L0_attn_x"], f32))
    wq = np.asarray(inputs["L0_attn_wq"], f32)
    wk = np.asarray(inputs["L0_attn_wk"], f32)
    wv = np.asarray(inputs["L0_attn_wv"], f32)
    wo = np.asarray(inputs["L0_attn_wo"], f32)
    bw = np.asarray(inputs["L0_ffn_base_w"], f32)
    bb = np.asarray(inputs["L0_ffn_base_b"], f32)
    A = np.asarray(inputs["L0_ffn_A"], f32)
    Bm = np.asarray(inputs["L0_ffn_B"], f32)
    rw = np.asarray(inputs["L0_ffn_router_w"], f32)

    WQ_h = np.ascontiguousarray(
        wq.reshape(32, 128, 32, 128).transpose(2, 1, 0, 3)).reshape(32, 128, 4096)
    WK_h = np.ascontiguousarray(
        wk.reshape(32, 128, 8, 128).transpose(2, 1, 0, 3)).reshape(8, 128, 4096)
    WV_h = np.ascontiguousarray(
        wv.reshape(32, 128, 8, 128).transpose(2, 1, 0, 3)).reshape(8, 128, 4096)
    WKV_h = np.ascontiguousarray(np.concatenate([WK_h, WV_h], 0))
    WO_h = np.ascontiguousarray(
        wo.reshape(32, 128, 32, 128).transpose(2, 1, 0, 3)).reshape(32, 128, 4096)
    BWT_h = np.ascontiguousarray(
        bw.reshape(32, 128, 32, 128).transpose(0, 3, 2, 1)).reshape(32, 128, 4096)
    BF_h = np.ascontiguousarray(
        Bm.transpose(0, 2, 1).reshape(8, 128, 32, 128).transpose(2, 1, 0, 3)
    ).reshape(32, 128, 1024)
    FFB_h = np.ascontiguousarray(np.concatenate([BWT_h, BF_h], axis=2))
    art = np.concatenate([A.T, rw.T], axis=1)
    ART_h = np.ascontiguousarray(
        art.reshape(32, 128, 80).transpose(1, 0, 2)).reshape(128, 2560)

    M1_h = np.zeros((16, 1024), f32)
    pp = np.arange(1024)
    M1_h[2 * (pp // 128) + (pp % 128) // 64, pp] = 1.0
    M2_h = np.zeros((64, 128), f32)
    M2_h[np.arange(128) % 64, np.arange(128)] = 1.0
    BB_h = np.ascontiguousarray(bb.reshape(32, 128).T)
    ID_h = np.eye(128, dtype=f32)

    bf = lambda a: np.ascontiguousarray(a.astype(BF16_NP))
    shared = dict(WKV=bf(WKV_h), WQ=bf(WQ_h), WOD=bf(WO_h), FFB=bf(FFB_h),
                  ARTW=bf(ART_h), M1D=M1_h, M2D=M2_h, BBD=BB_h, IDD=ID_h)

    in_maps = []
    meta = []
    for b in range(4):
        xT = np.ascontiguousarray(x[b].T)
        XT_h = bf(np.ascontiguousarray(
            xT.reshape(32, 128, 4, 512).transpose(2, 1, 0, 3)))
        for role in range(2):
            ch = CHUNKS[role]
            cols = np.concatenate([np.arange(c * 256, (c + 1) * 256) for c in ch])
            xown = xT[:, cols]
            XOWN_h = np.ascontiguousarray(
                xown.reshape(32, 128, 1024).transpose(1, 0, 2))
            m4 = np.zeros((4, 4, 128, 256), f32)
            for s in range(4):
                q0 = ch[s] * 256
                for j in range(4):
                    kv0 = (4 * s + j) * 128
                    kvi = kv0 + np.arange(128)[:, None]
                    qi = q0 + np.arange(256)[None, :]
                    m4[s, j] = (kvi <= qi).astype(f32)
            MSK_h = np.ascontiguousarray(m4.transpose(2, 0, 1, 3)).reshape(128, 4096)
            im = dict(shared)
            im["XT"] = XT_h
            im["XOWN"] = XOWN_h
            im["MSK"] = bf(MSK_h)
            in_maps.append(im)
            meta.append((b, ch))
    return in_maps, meta


def kernel(**inputs):
    in_maps, meta = _host_prep(inputs)
    nc = build_program()
    res = run_bass_kernel_spmd(nc, in_maps, core_ids=list(range(8)))
    out = np.empty((4, 2048, 4096), np.float32)
    for i, r in enumerate(res.results):
        b, ch = meta[i]
        oT = np.asarray(r["OUT"])
        o = np.ascontiguousarray(oT.T)
        for k, c in enumerate(ch):
            out[b, c * 256:(c + 1) * 256] = o[k * 256:(k + 1) * 256]
    return out


# version 24
# speedup vs baseline: 1.0720x; 1.0720x over previous
import numpy as np
import ml_dtypes
from contextlib import ExitStack

BF16_NP = ml_dtypes.bfloat16

import concourse.bass as bass
import concourse.mybir as mybir
from concourse.bass_utils import run_bass_kernel_spmd
from concourse.alu_op_type import AluOpType

F32 = mybir.dt.float32
BF16 = mybir.dt.bfloat16
AF = mybir.ActivationFunctionType
AX = mybir.AxisListType

NH, NKV, HD, D = 32, 8, 128, 4096
SCALE = float(HD) ** -0.5
EPS = 1e-5
CHUNKS = {0: [0, 3, 4, 7], 1: [1, 2, 5, 6]}
INV_D = 1.0 / 4096.0


class Ctr:
    def __init__(self, stack, nc, name):
        self.h = stack.enter_context(nc.semaphore(name))
        self.v = 0

    def inc(self, inst, k=1):
        inst.then_inc(self.h, k)
        self.v += k
        return self.v


def build_program(phases=4, nheads=32, do_av=True, do_norm=True, do_tr=True, do_spill=True, do_exp=True, do_mask=True, dump=False):
    nc = bass.Bass("TRN2", target_bir_lowering=False)

    XT = nc.dram_tensor("XT", [4, 128, 32, 512], BF16, kind="ExternalInput").ap()
    XOWN = nc.dram_tensor("XOWN", [128, 32, 1024], F32, kind="ExternalInput").ap()
    WKV = nc.dram_tensor("WKV", [16, 128, 4096], BF16, kind="ExternalInput").ap()
    WQ = nc.dram_tensor("WQ", [32, 128, 4096], BF16, kind="ExternalInput").ap()
    WOD = nc.dram_tensor("WOD", [32, 128, 4096], BF16, kind="ExternalInput").ap()
    FFB = nc.dram_tensor("FFB", [32, 128, 5120], BF16, kind="ExternalInput").ap()
    ARTW = nc.dram_tensor("ARTW", [128, 2560], BF16, kind="ExternalInput").ap()
    MSK = nc.dram_tensor("MSK", [128, 4096], BF16, kind="ExternalInput").ap()
    M1D = nc.dram_tensor("M1D", [16, 1024], F32, kind="ExternalInput").ap()
    M2D = nc.dram_tensor("M2D", [64, 128], F32, kind="ExternalInput").ap()
    BBD = nc.dram_tensor("BBD", [128, 32], F32, kind="ExternalInput").ap()
    IDD = nc.dram_tensor("IDD", [128, 128], F32, kind="ExternalInput").ap()
    OUT = nc.dram_tensor("OUT", [4096, 1024], F32, kind="ExternalOutput").ap()
    OSC = nc.dram_tensor("OSC", [128, 32768], BF16, kind="Internal").ap()
    if dump:
        DBG1 = nc.dram_tensor("DBG1", [128, 32896], BF16, kind="ExternalOutput").ap()
        DBG2 = nc.dram_tensor("DBG2", [128, 32768], BF16, kind="ExternalOutput").ap()
        DBG3 = nc.dram_tensor("DBG3", [128, 32768], BF16, kind="ExternalOutput").ap()
        DBG4 = nc.dram_tensor("DBG4", [128, 32768], BF16, kind="ExternalOutput").ap()
        DBG5 = nc.dram_tensor("DBG5", [128, 8192], BF16, kind="ExternalOutput").ap()
        DBG6 = nc.dram_tensor("DBG6", [64, 1024], BF16, kind="ExternalOutput").ap()
        DBG7 = nc.dram_tensor("DBG7", [16, 1024], F32, kind="ExternalOutput").ap()
        DBG8 = nc.dram_tensor("DBG8", [16, 1024], BF16, kind="ExternalOutput").ap()
        DBG9 = nc.dram_tensor("DBG9", [128, 32768], BF16, kind="ExternalOutput").ap()
        RD1 = nc.dram_tensor("RD1", [128, 128], F32, kind="ExternalOutput").ap()
        RD2 = nc.dram_tensor("RD2", [128, 128], F32, kind="ExternalOutput").ap()
        RD3 = nc.dram_tensor("RD3", [128, 64], F32, kind="ExternalOutput").ap()

    stack = ExitStack()
    with stack:
        E = stack.enter_context

        REG1 = E(nc.sbuf_tensor("REG1", [128, 32896], BF16))
        REG2 = E(nc.sbuf_tensor("REG2", [128, 32768], BF16))
        WST = E(nc.sbuf_tensor("WST", [128, 3 * 5120], BF16))
        IDB = E(nc.sbuf_tensor("IDB", [128, 128], BF16))
        IDF = E(nc.sbuf_tensor("IDF", [128, 128], F32))
        ONE4K = E(nc.sbuf_tensor("ONE4K", [128, 128], BF16))
        EPSB = E(nc.sbuf_tensor("EPSB", [128, 1], F32))

        KT = lambda g, a, b: REG1[:, 2048 * g + a: 2048 * g + b]
        VA = lambda kt, g: REG1[:, 16384 + (kt * 8 + g) * 129: 16384 + (kt * 8 + g) * 129 + 129]
        VAall = REG1[:, 16384:32896]

        def wslot(n, w=5120):
            i = n % 3
            return WST[:, 5120 * i: 5120 * i + w]

        p2 = ExitStack()
        MASKS = p2.enter_context(nc.sbuf_tensor("MASKS", [128, 4096], BF16))
        QT = p2.enter_context(nc.sbuf_tensor("QT", [128, 2 * 1024], BF16))
        UB = p2.enter_context(nc.sbuf_tensor("UB", [128, 2 * 1024], BF16))
        OTH = p2.enter_context(nc.sbuf_tensor("OTH", [128, 2 * 1024], BF16))
        OPR = p2.enter_context(nc.sbuf_tensor("OPR", [128, 2 * 256], BF16))
        RDN = p2.enter_context(nc.sbuf_tensor("RDN", [128, 4], F32))
        p1 = ExitStack()
        VTS = p1.enter_context(nc.sbuf_tensor("VTS", [128, 2 * 1024], BF16))

        PSA = E(nc.psum_tensor("PSA", [128, 2048], F32))
        PSB = E(nc.psum_tensor("PSB", [128, 1024], F32))
        PST1T = E(nc.psum_tensor("PST1", [128, 1024], BF16))
        PST2T = E(nc.psum_tensor("PST2", [128, 1024], BF16))
        PST1 = PST1T[:, 0:128]
        PST2 = PST2T[:, 0:128]

        ccon = Ctr(stack, nc, "ccon")
        csyc = Ctr(stack, nc, "csyc")
        cw = Ctr(stack, nc, "cw")
        cwr = Ctr(stack, nc, "cwr")
        cx = Ctr(stack, nc, "cx")
        cxr = Ctr(stack, nc, "cxr")
        ckv = Ctr(stack, nc, "ckv")
        ckve = Ctr(stack, nc, "ckve")
        ctr = Ctr(stack, nc, "ctr")
        ctre = Ctr(stack, nc, "ctre")
        cpk = Ctr(stack, nc, "cpk")
        cpkr = Ctr(stack, nc, "cpkr")
        cqf = Ctr(stack, nc, "cqf")
        cexp = Ctr(stack, nc, "cexp")
        cmsk = Ctr(stack, nc, "cmsk")
        cur = Ctr(stack, nc, "cur")
        cav = Ctr(stack, nc, "cav")
        cnrm = Ctr(stack, nc, "cnrm")
        cosp = Ctr(stack, nc, "cosp")
        cxf = Ctr(stack, nc, "cxf")
        cwo = Ctr(stack, nc, "cwo")
        chh = Ctr(stack, nc, "chh")
        cxo = Ctr(stack, nc, "cxo")
        cxor = Ctr(stack, nc, "cxor")
        csq = Ctr(stack, nc, "csq")
        cst = Ctr(stack, nc, "cst")
        cstm = Ctr(stack, nc, "cstm")
        cfin = Ctr(stack, nc, "cfin")
        cx1 = Ctr(stack, nc, "cx1")
        crld = Ctr(stack, nc, "crld")
        cart = Ctr(stack, nc, "cart")
        crt = Ctr(stack, nc, "crt")
        crte = Ctr(stack, nc, "crte")
        cct = Ctr(stack, nc, "cct")
        ccte = Ctr(stack, nc, "ccte")
        cffn = Ctr(stack, nc, "cffn")
        cffb = Ctr(stack, nc, "cffb")
        cpkq = Ctr(stack, nc, "cpkq")
        cx1g = Ctr(stack, nc, "cx1g")
        csqv = Ctr(stack, nc, "csqv")
        ch2 = Ctr(stack, nc, "ch2")
        cout = Ctr(stack, nc, "cout")
        coutg = Ctr(stack, nc, "coutg")

        blk = E(nc.Block())

        def emit_constants():
            @blk.gpsimd
            def _(g):
                ccon.inc(g.dma_start(MASKS[:, :], MSK), 16)
                ccon.inc(g.dma_start(IDB[:, :], IDD), 16)
                g.memset(VAall, 1.0)
                g.memset(EPSB[:, :], EPS)
                ccon.inc(g.memset(ONE4K[:, :], INV_D), 1)

        @blk.sync
        def _(s_):
            csyc.inc(s_.dma_start(IDF[:, :], IDD), 16)

        csyc_idf = csyc.v

        def xstage(cc):
            i = cc % 2
            return REG2[:, 16384 * i: 16384 * i + 16384]

        nblk = 0
        wslot_guard = []
        kv_evt = 0
        vts_free_at = [0, 0]
        pst_free_at = [0, 0]
        tr_idx = 0

        for cc in range(4):
            @blk.gpsimd
            def _(g, cc=cc):
                if cc >= 2:
                    g.wait_ge(ckv.h, 16 * (cc - 1))
                cx.inc(g.dma_start(xstage(cc), XT[cc]), 16)

            for gg in range(16):
                n = nblk
                nblk += 1

                @blk.gpsimd
                def _(g, n=n, gg=gg):
                    if n >= 3:
                        gd = wslot_guard[n - 3]
                        g.wait_ge(gd[0], gd[1])
                    cw.inc(g.dma_start(wslot(n, 4096), WKV[gg]), 16)

                if cc == 0 and gg == 0:
                    emit_constants()
                    ccon_ready = ccon.v

                ev = kv_evt
                kv_evt += 1
                r = ev % 4
                wslot_guard.append((ckv.h, ev + 1))

                @blk.tensor
                def _(t, n=n, ci=cc, ev=ev, r=r, gg=gg):
                    t.wait_ge(cw.h, 16 * (n + 1))
                    t.wait_ge(cx.h, 16 * (ci + 1))
                    if ev >= 4:
                        t.wait_ge(ckve.h, ev - 3)
                    for ct in range(32):
                        mm = t.matmul(
                            PSA[:, 512 * r: 512 * r + 512],
                            wslot(n, 4096)[:, 128 * ct: 128 * ct + 128],
                            xstage(cc)[:, 512 * ct: 512 * ct + 512],
                            start=(ct == 0), stop=(ct == 31),
                        )
                    ckv.inc(mm)

                if gg < 8:
                    @blk.vector
                    def _(v, ev=ev, r=r, g_=gg, cc=cc):
                        v.wait_ge(ckv.h, ev + 1)
                        ckve.inc(v.tensor_copy(
                            KT(g_, 512 * cc, 512 * cc + 512),
                            PSA[:, 512 * r: 512 * r + 512]))
                else:
                    vslot = ev % 2
                    need_tr = vts_free_at[vslot]

                    @blk.vector
                    def _(v, ev=ev, r=r, vslot=vslot, need_tr=need_tr):
                        v.wait_ge(ckv.h, ev + 1)
                        if need_tr:
                            v.wait_ge(ctr.h, need_tr)
                        ckve.inc(v.tensor_copy(
                            VTS[:, 1024 * vslot: 1024 * vslot + 512],
                            PSA[:, 512 * r: 512 * r + 512]))

                    my_evac = ckve.v
                    for tsub in range(4):
                        ti = tr_idx
                        tr_idx += 1
                        tp = ti % 2
                        PSTt = PST1 if tp == 0 else PST2
                        need_ev = pst_free_at[tp]

                        @blk.tensor
                        def _(t, tp=tp, PSTt=PSTt, vslot=vslot, ti=ti,
                              tsub=tsub, need_ev=need_ev, my_evac=my_evac):
                            t.wait_ge(ckve.h, my_evac)
                            if ti == 0:
                                t.wait_ge(ccon.h, 32)
                            if need_ev:
                                t.wait_ge(ctre.h, need_ev)
                            ctr.inc(t.transpose(
                                PSTt,
                                VTS[:, 1024 * vslot + 128 * tsub: 1024 * vslot + 128 * tsub + 128],
                                IDB[:, :]))

                        kt_g = 4 * cc + tsub
                        g_ = gg - 8

                        @blk.vector
                        def _(v, ti=ti, PSTt=PSTt, kt_g=kt_g, g_=g_):
                            v.wait_ge(ctr.h, ti + 1)
                            if ti == 0:
                                v.wait_ge(ccon.h, ccon_ready)
                            ctre.inc(v.tensor_copy(
                                VA(kt_g, g_)[:, 0:128], PSTt))

                        pst_free_at[tp] = ctre.v
                    vts_free_at[vslot] = ctr.v

        @blk.gpsimd
        def _(g):
            g.wait_ge(ckv.h, 48)
            cx.inc(g.dma_start(REG2[:, 0:16384], XOWN[:, 0:16, :]), 16)
        xown_r0 = cx.v

        @blk.gpsimd
        def _(g):
            g.wait_ge(ckv.h, ckv.v)
            cx.inc(g.dma_start(REG2[:, 16384:24576], XOWN[:, 16:24, :]), 16)
        xown_r1 = cx.v

        @blk.gpsimd
        def _(g):
            cx.inc(g.dma_start(REG2[:, 24576:32768], XOWN[:, 24:32, :]), 16)
        xown_r2 = cx.v

        p1.close()
        if phases < 2:
            p2.close()
            return nc

        pk = 0
        u_idx = 0
        av_idx = 0
        m_idx = 0
        kvdone = ckve.v
        va_done = ctre.v
        if dump:
            @blk.sync
            def _(s_):
                s_.wait_ge(ckve.h, kvdone)
                s_.wait_ge(ctre.h, va_done)
                cosp.inc(s_.dma_start(DBG1, REG1[:, :]), 16)

        q31_fill = [0]
        head_lastS = {}
        pack_rel = []

        def pack_off(p):
            return 1024 * (p % 2)

        qt_slice = lambda h, a, b: QT[:, 1024 * (h % 2) + a: 1024 * (h % 2) + b]

        def emit_qproj(h):
            nonlocal pk, nblk
            n = nblk
            nblk += 1

            @blk.gpsimd
            def _(g, n=n, h=h):
                gd = wslot_guard[n - 3]
                g.wait_ge(gd[0], gd[1])
                cw.inc(g.dma_start(wslot(n, 4096), WQ[h]), 16)

            mypk = pk
            pk += 1
            off = pack_off(mypk)
            wslot_guard.append((cpk.h, mypk + 1))

            @blk.tensor
            def _(t, n=n, mypk=mypk, off=off, h=h):
                t.wait_ge(cw.h, 16 * (n + 1))
                if mypk >= 2:
                    gd = pack_rel[mypk - 2]
                    t.wait_ge(gd[0], gd[1])
                for half in range(2):
                    for ct in range(32):
                        if h == 0 and half == 0:
                            if ct == 0:
                                t.wait_ge(cx.h, xown_r0)
                            elif ct == 16:
                                t.wait_ge(cx.h, xown_r1)
                            elif ct == 24:
                                t.wait_ge(cx.h, xown_r2)
                        mm = t.matmul(
                            PSA[:, off + 512 * half: off + 512 * half + 512],
                            wslot(n, 4096)[:, 128 * ct: 128 * ct + 128],
                            REG2[:, 1024 * ct + 512 * half: 1024 * ct + 512 * half + 512],
                            start=(ct == 0), stop=(ct == 31),
                        )
                cpk.inc(mm)
                if h == 31:
                    q31_fill[0] = cpk.v

            @blk.vector
            def _(v, mypk=mypk, off=off, h=h):
                v.wait_ge(cpk.h, mypk + 1)
                if h >= 2:
                    v.wait_ge(cpk.h, head_lastS[h - 2])
                v.tensor_copy(qt_slice(h, 0, 512), PSA[:, off: off + 512])
                cpkq.inc(v.tensor_copy(qt_slice(h, 512, 1024), PSA[:, off + 512: off + 1024]))
            pack_rel.append((cpkq.h, cpkq.v))

        from collections import deque
        workq = deque()
        last_oth = []
        exp_jobs = []
        tr_base = ctr.v
        tr_created = 0
        av_created = 0
        round_tr_last = {}
        pst_guard = [0, 0]
        ctre_assign = [ctre.v]

        def flush(n_keep=2):
            while len(workq) > n_keep:
                workq.popleft()()

        def emit_attn(h):
            nonlocal pk, u_idx, av_idx, m_idx, tr_created, av_created
            g_ = h // 4
            for s in range(4):
                myav = av_idx
                av_idx += 1
                a_off = 512 * (myav % 2)
                npacks = s + 1
                for p in range(npacks):
                    mypk = pk
                    pk += 1
                    myu = u_idx
                    u_idx += 1
                    off = pack_off(mypk)
                    masked = (p == s)

                    @blk.tensor
                    def _(t, mypk=mypk, off=off, s=s, p=p, h=h, g_=g_):
                        if mypk >= 2:
                            gd = pack_rel[mypk - 2]
                            t.wait_ge(gd[0], gd[1])
                        if h == 0 and s == 0 and p == 0:
                            t.wait_ge(ckve.h, kvdone)
                        for j in range(4):
                            kt = 4 * p + j
                            mm = t.matmul(
                                PSA[:, off + 256 * j: off + 256 * j + 256],
                                KT(g_, 128 * kt, 128 * kt + 128),
                                qt_slice(h, 256 * s, 256 * s + 256),
                                start=True, stop=True,
                            )
                        cpk.inc(mm)

                    exp_jobs.append((mypk, myu, off))
                    pack_rel.append((cpkr.h, len(exp_jobs)))
                    gd_pack = pack_rel[-1]

                    if masked:
                        m_idx += 1
                        mym = m_idx

                        @blk.vector
                        def _(v, gd=gd_pack, myu=myu, s=s):
                            v.wait_ge(gd[0], gd[1])
                            u = UB[:, 1024 * (myu % 2): 1024 * (myu % 2) + 1024]
                            cmsk.inc(v.tensor_tensor(
                                u, u, MASKS[:, 1024 * s: 1024 * s + 1024],
                                op=AluOpType.mult))
                    else:
                        mym = m_idx

                    av_created += 1
                    my_cur_val = av_created
                    first_av = (p == 0)

                    def av_item(myu=myu, masked=masked, mym=mym, p=p, s=s,
                                g_=g_, a_off=a_off, first_av=first_av,
                                myav=myav, gd=gd_pack):
                        @blk.tensor
                        def _(t):
                            if masked:
                                t.wait_ge(cmsk.h, mym)
                            else:
                                t.wait_ge(gd[0], gd[1])
                            if first_av:
                                if myav == 0:
                                    t.wait_ge(ctre.h, va_done)
                                if myav >= 2:
                                    t.wait_ge(cnrm.h, myav - 1)
                            u0 = 1024 * (myu % 2)
                            first = first_av
                            for j in range(4):
                                kt = 4 * p + j
                                for qs in range(2):
                                    mm = t.matmul(
                                        PSB[:, a_off + 129 * qs: a_off + 129 * qs + 129],
                                        UB[:, u0 + 256 * j + 128 * qs: u0 + 256 * j + 128 * qs + 128],
                                        VA(kt, g_),
                                        start=first, stop=(p == s and j == 3 and qs == 1),
                                    )
                                    first = False
                            cur.inc(mm)
                    workq.append(av_item)
                    flush()

                opr_gd = round_tr_last.get(myav - 2, 0)

                @blk.vector
                def _(v, myav=myav, a_off=a_off, avval=my_cur_val, opr_gd=opr_gd):
                    v.wait_ge(cur.h, avval)
                    if opr_gd:
                        v.wait_ge(ctr.h, opr_gd)
                    o = 256 * (myav % 2)
                    v.reciprocal(RDN[:, 0:1], PSB[:, a_off + 128: a_off + 129])
                    v.reciprocal(RDN[:, 1:2], PSB[:, a_off + 257: a_off + 258])
                    v.drain()
                    v.tensor_scalar(OPR[:, o: o + 128], PSB[:, a_off: a_off + 128],
                                    RDN[:, 0:1], None, op0=AluOpType.mult)
                    cnrm.inc(v.tensor_scalar(
                        OPR[:, o + 128: o + 256], PSB[:, a_off + 129: a_off + 257],
                        RDN[:, 1:2], None, op0=AluOpType.mult))
                mynrm = cnrm.v

                for qs in range(2):
                    ti = tr_created
                    tr_created += 1
                    tp = ti % 2
                    PSTt = PST1 if tp == 0 else PST2
                    need_ev = pst_guard[tp]
                    ctre_assign[0] += 1
                    my_ctre = ctre_assign[0]
                    pst_guard[tp] = my_ctre

                    def tr_item(myav=myav, qs=qs, PSTt=PSTt, need_ev=need_ev,
                                mynrm=mynrm, ti=ti, h=h, s=s):
                        @blk.tensor
                        def _(t):
                            t.wait_ge(cnrm.h, mynrm)
                            if need_ev:
                                t.wait_ge(ctre.h, need_ev)
                            o = 256 * (myav % 2)
                            ctr.inc(t.transpose(
                                PSTt, OPR[:, o + 128 * qs: o + 128 * qs + 128],
                                IDB[:, :]))

                        @blk.vector
                        def _(v):
                            v.wait_ge(ctr.h, tr_base + ti + 1)
                            if h >= 2 and s == 0 and qs == 0:
                                v.wait_ge(cosp.h, 16 * (h - 1))
                            ctre.inc(v.tensor_copy(
                                OTH[:, 1024 * (h % 2) + 256 * s + 128 * qs:
                                     1024 * (h % 2) + 256 * s + 128 * qs + 128],
                                PSTt))
                    workq.append(tr_item)
                    flush()
                round_tr_last[myav] = tr_base + tr_created

            head_lastS[h] = cpk.v

            myctre = ctre_assign[0]

            if h < nheads - 2:
                @blk.sync
                def _(s_, h=h, myctre=myctre):
                    s_.wait_ge(ctre.h, myctre)
                    cosp.inc(s_.dma_start(
                        OSC[:, 1024 * h: 1024 * h + 1024],
                        OTH[:, 1024 * (h % 2): 1024 * (h % 2) + 1024]), 16)
            else:
                last_oth.append((h, myctre))

        emit_qproj(0)
        emit_qproj(1)
        for h in range(nheads):
            emit_attn(h)
            if h + 2 < nheads:
                emit_qproj(h + 2)
        flush(0)

        @blk.scalar
        def _(sc):
            for i, (mypk, myu, off) in enumerate(exp_jobs):
                sc.wait_ge(cpk.h, mypk + 1)
                if myu >= 2:
                    sc.wait_ge(cur.h, myu - 1)
                act = sc.activation(
                    UB[:, 1024 * (myu % 2): 1024 * (myu % 2) + 1024],
                    PSA[:, off: off + 1024],
                    AF.Exp, scale=SCALE)
                act.then_inc(cpkr.h, 1)
        cpkr.v += len(exp_jobs)

        if nheads == 32:
            @blk.sync
            def _(s_):
                s_.wait_ge(cosp.h, cosp.v)
                s_.wait_ge(cpk.h, q31_fill[0])
                cosp.inc(s_.dma_start(REG2[:, 0:30720], OSC[:, 0:30720]), 16)

            for h, myctre in last_oth:
                @blk.vector
                def _(v, h=h, myctre=myctre):
                    v.wait_ge(ctre.h, myctre)
                    v.wait_ge(cpk.h, q31_fill[0])
                    crld.inc(v.tensor_copy(
                        REG2[:, 1024 * h: 1024 * h + 1024],
                        OTH[:, 1024 * (h % 2): 1024 * (h % 2) + 1024]))
        ot_ready = cosp.v
        oth_copied = crld.v
        if dump and nheads == 32:
            @blk.sync
            def _(s_):
                s_.wait_ge(cosp.h, ot_ready)
                cosp.inc(s_.dma_start(DBG2, REG2[:, :]), 16)
        ot_ready = cosp.v

        p2.close()
        if phases < 3:
            return nc

        p3 = ExitStack()
        SQ = p3.enter_context(nc.sbuf_tensor("SQ", [128, 2 * 512], BF16))
        SD = p3.enter_context(nc.sbuf_tensor("SD", [128, 512], F32))
        M2SB = p3.enter_context(nc.sbuf_tensor("M2SB", [128, 512], F32))
        RSTD2 = p3.enter_context(nc.sbuf_tensor("RSTD2", [128, 1024], BF16))
        BT2 = p3.enter_context(nc.sbuf_tensor("BT2", [128, 1024], BF16))
        p3a = ExitStack()
        XO = p3a.enter_context(nc.sbuf_tensor("XO", [128, 3 * 1024], F32))

        st_sum = [0, 1024]
        st_sq = [512, 1536]

        def emit_ln_finalize(tc, mystats, wide=True):
            sum_off, sq_off = st_sum[tc], st_sq[tc]
            RS_ = RSTD2[:, 512 * tc: 512 * tc + 512]
            BT_ = BT2[:, 512 * tc: 512 * tc + 512]

            @blk.vector
            def _(v, sum_off=sum_off, sq_off=sq_off, mystats=mystats):
                v.wait_ge(cstm.h, mystats)
                v.tensor_copy(M2SB[:, :], PSA[:, sum_off: sum_off + 512])
                v.drain()
                v.tensor_tensor(M2SB[:, :], M2SB[:, :], M2SB[:, :], op=AluOpType.mult)
                csq.inc(v.tensor_tensor(M2SB[:, :], PSA[:, sq_off: sq_off + 512],
                                        M2SB[:, :], op=AluOpType.subtract))
            myv = csq.v

            @blk.scalar
            def _(sc, myv=myv):
                sc.wait_ge(csq.h, myv)
                csq.inc(sc.activation(SD[:, :], M2SB[:, :], AF.Sqrt, bias=EPSB[:, 0:1]))
            mysd = csq.v

            @blk.vector
            def _(v, sum_off=sum_off, mysd=mysd, RS_=RS_, BT_=BT_):
                v.wait_ge(csq.h, mysd)
                with nc.allow_low_precision(reason="bf16 rstd, validated margin"):
                    v.reciprocal(RS_, SD[:, :])
                    v.drain()
                    cfin.inc(v.tensor_tensor(BT_, PSA[:, sum_off: sum_off + 512],
                                             RS_, op=AluOpType.mult))

        wo_rounds = 0
        pending_stats = []

        def flush_stats(keep=1):
            while len(pending_stats) > keep:
                pending_stats.pop(0)()

        for ot in range(32):
            n = nblk
            nblk += 1
            wslot_guard.append((cwo.h, 2 * (ot + 1)))

            @blk.gpsimd
            def _(g, n=n, ot=ot):
                gd = wslot_guard[n - 3]
                g.wait_ge(gd[0], gd[1])
                cw.inc(g.dma_start(wslot(n, 4096), WOD[ot]), 16)

            @blk.sync
            def _(s_, ot=ot):
                if ot >= 3:
                    s_.wait_ge(chh.h, 2 * ot - 4)
                elif ot == 0:
                    s_.wait_ge(crld.h, oth_copied)
                cxo.inc(s_.dma_start(XO[:, 1024 * (ot % 3): 1024 * (ot % 3) + 1024],
                                     XOWN[:, ot, :]), 16)

            for tc in range(2):
                wo_rounds += 1
                rnd = wo_rounds
                rr = (rnd - 1) % 2
                b_off = 512 * rr

                @blk.tensor
                def _(t, n=n, ot=ot, tc=tc, b_off=b_off, rnd=rnd):
                    t.wait_ge(cw.h, 16 * (n + 1))
                    if ot == 0 and tc == 0:
                        t.wait_ge(cosp.h, ot_ready)
                        t.wait_ge(crld.h, oth_copied)
                    if rnd > 2:
                        t.wait_ge(chh.h, rnd - 2)
                    for ht in range(32):
                        mm = t.matmul(
                            PSB[:, b_off: b_off + 512],
                            wslot(n, 4096)[:, 128 * ht: 128 * ht + 128],
                            REG2[:, 1024 * ht + 512 * tc: 1024 * ht + 512 * tc + 512],
                            start=(ht == 0), stop=(ht == 31),
                        )
                    cwo.inc(mm)
                flush_stats()

                mywo = cwo.v

                @blk.vector
                def _(v, ot=ot, tc=tc, b_off=b_off, mywo=mywo):
                    v.wait_ge(cwo.h, mywo)
                    v.wait_ge(cxo.h, 16 * (ot + 1))
                    mm = v.tensor_tensor(
                        REG1[:, 1024 * ot + 512 * tc: 1024 * ot + 512 * tc + 512],
                        PSB[:, b_off: b_off + 512],
                        XO[:, 1024 * (ot % 3) + 512 * tc: 1024 * (ot % 3) + 512 * tc + 512],
                        op=AluOpType.add)
                    chh.inc(mm)

                myhh = chh.v

                @blk.vector
                def _(v, ot=ot, tc=tc, rnd=rnd, rr=rr):
                    if rnd > 2:
                        v.wait_ge(cstm.h, rnd - 2)
                    sl = REG1[:, 1024 * ot + 512 * tc: 1024 * ot + 512 * tc + 512]
                    csqv.inc(v.tensor_tensor(
                        SQ[:, 512 * rr: 512 * rr + 512], sl, sl, op=AluOpType.mult))

                mysq = csqv.v

                def stats_item(ot=ot, tc=tc, mysq=mysq, rr=rr):
                    @blk.tensor
                    def _(t):
                        t.wait_ge(csqv.h, mysq)
                        t.matmul(PSA[:, st_sum[tc]: st_sum[tc] + 512], ONE4K[:, :],
                                 REG1[:, 1024 * ot + 512 * tc: 1024 * ot + 512 * tc + 512],
                                 start=(ot == 0), stop=(ot == 31))
                        mm = t.matmul(PSA[:, st_sq[tc]: st_sq[tc] + 512], ONE4K[:, :],
                                      SQ[:, 512 * rr: 512 * rr + 512],
                                      start=(ot == 0), stop=(ot == 31))
                        cstm.inc(mm)
                pending_stats.append(stats_item)
        flush_stats(0)

        emit_ln_finalize(0, cstm.v)
        fin0 = cfin.v
        emit_ln_finalize(1, cstm.v)
        fin1 = cfin.v
        x1v_base = cx1.v
        for tc in range(2):
            myfin = fin0 if tc == 0 else fin1
            for ot in range(32):
                @blk.vector
                def _(v, ot=ot, tc=tc, myfin=myfin):
                    v.wait_ge(cfin.h, myfin)
                    v.tensor_tensor(SQ[:, 512 * tc: 512 * tc + 512],
                                    REG1[:, 1024 * ot + 512 * tc: 1024 * ot + 512 * tc + 512],
                                    RSTD2[:, 512 * tc: 512 * tc + 512],
                                    op=AluOpType.mult)
                    cx1.inc(v.tensor_tensor(
                        REG2[:, 1024 * ot + 512 * tc: 1024 * ot + 512 * tc + 512],
                        SQ[:, 512 * tc: 512 * tc + 512], BT2[:, 512 * tc: 512 * tc + 512],
                        op=AluOpType.subtract))
        x1_done = cx1.v
        if dump:
            @blk.sync
            def _(s_):
                s_.wait_ge(cx1.h, x1_done)
                cosp.inc(s_.dma_start(DBG3, REG2[:, 0:32768]), 16)
                cosp.inc(s_.dma_start(DBG4, REG1[:, 0:32768]), 16)
        if phases < 4:
            p3a.close()
            p3.close()
            return nc

        p3a.close()
        p4 = ExitStack()
        CT = p4.enter_context(nc.sbuf_tensor("CT", [128, 8192], BF16))
        RH = p4.enter_context(nc.sbuf_tensor("RH", [64, 1024], BF16))
        LGT = p4.enter_context(nc.sbuf_tensor("LGT", [16, 1024], F32))
        LGtm = p4.enter_context(nc.sbuf_tensor("LGtm", [128, 128], F32))
        CMB = p4.enter_context(nc.sbuf_tensor("CMB", [16, 1024], BF16))
        M1S = p4.enter_context(nc.sbuf_tensor("M1S", [16, 1024], BF16))
        M2S = p4.enter_context(nc.sbuf_tensor("M2S", [64, 128], BF16))
        BBS = p4.enter_context(nc.sbuf_tensor("BBS", [128, 32], F32))
        ETMP = p4.enter_context(nc.sbuf_tensor("ETMP", [128, 16], F32))
        PTMP = p4.enter_context(nc.sbuf_tensor("PTMP", [128, 16], F32))
        GTMP = p4.enter_context(nc.sbuf_tensor("GTMP", [128, 24], F32))
        SCL = p4.enter_context(nc.sbuf_tensor("SCL", [128, 8], F32))
        ETM8 = p4.enter_context(nc.sbuf_tensor("ETM8", [128, 128], F32))
        CMBtm = LGtm
        RDB1 = p4.enter_context(nc.sbuf_tensor("RDB1", [128, 128], F32))
        RDB2 = p4.enter_context(nc.sbuf_tensor("RDB2", [128, 128], F32))
        RDB3 = p4.enter_context(nc.sbuf_tensor("RDB3", [128, 64], F32))
        OST = p4.enter_context(nc.sbuf_tensor("OST", [128, 4 * 512], F32))

        @blk.sync
        def _(s_):
            s_.wait_ge(chh.h, 64)
            csyc.inc(s_.dma_start(BBS[:, :], BBD), 16)
        bb_ready = csyc.v

        n_art = nblk
        nblk += 1
        wslot_guard.append((cart.h, 1))

        @blk.gpsimd
        def _(g, n=n_art):
            gd = wslot_guard[n - 3]
            g.wait_ge(gd[0], gd[1])
            cw.inc(g.dma_start(wslot(n, 2560), ARTW), 16)

        n_ffb0 = nblk
        nblk += 1
        wslot_guard.append((cffn.h, 1))

        @blk.gpsimd
        def _(g, n=n_ffb0):
            gd = wslot_guard[n - 3]
            g.wait_ge(gd[0], gd[1])
            cw.inc(g.dma_start(wslot(n, 5120), FFB[0]), 16)

        n_ffb1 = nblk
        nblk += 1
        wslot_guard.append((cffn.h, 2))

        @blk.gpsimd
        def _(g, n=n_ffb1):
            gd = wslot_guard[n - 3]
            g.wait_ge(gd[0], gd[1])
            cw.inc(g.dma_start(wslot(n, 5120), FFB[1]), 16)

        @blk.gpsimd
        def _(g):
            g.wait_ge(chh.h, 64)
            ccon.inc(g.dma_start(M1S[:, :], M1D), 16)
            ccon.inc(g.dma_start(M2S[:, :], M2D), 16)
        m_ready = ccon.v

        @blk.tensor
        def _(t, n=n_ffb0):
            t.wait_ge(cw.h, 16 * (n + 1))
            for ct in range(32):
                t.wait_ge(cx1.h, x1v_base + ct + 1)
                t.matmul(PSB[:, 0:512],
                         wslot(n, 5120)[:, 128 * ct: 128 * ct + 128],
                         REG2[:, 1024 * ct: 1024 * ct + 512],
                         start=(ct == 0), stop=False)

        @blk.tensor
        def _(t, n=n_ffb1):
            t.wait_ge(cw.h, 16 * (n + 1))
            for ct in range(32):
                t.matmul(PSB[:, 512:1024],
                         wslot(n, 5120)[:, 128 * ct: 128 * ct + 128],
                         REG2[:, 1024 * ct: 1024 * ct + 512],
                         start=(ct == 0), stop=False)

        @blk.tensor
        def _(t, n=n_art, x1_done=x1_done):
            t.wait_ge(cw.h, 16 * (n + 1))
            t.wait_ge(cx1.h, x1_done)
            for half in range(2):
                for ct in range(32):
                    mm = t.matmul(
                        PSA[0:80, 512 * half: 512 * half + 512],
                        wslot(n, 2560)[:, 80 * ct: 80 * ct + 80],
                        REG2[:, 1024 * ct + 512 * half: 1024 * ct + 512 * half + 512],
                        start=(ct == 0), stop=(ct == 31),
                    )
            cart.inc(mm)
        art_mm = cart.v

        @blk.scalar
        def _(sc, art_mm=art_mm):
            sc.wait_ge(cart.h, art_mm)
            sc.copy(RH[:, :], PSA[0:64, 0:1024])
            cart.inc(sc.copy(LGT[:, :], PSA[64:80, 0:1024]))
        art_ev = cart.v

        @blk.tensor
        def _(t, art_ev=art_ev):
            t.wait_ge(cart.h, art_ev)
            t.wait_ge(csyc.h, csyc_idf)
            for tt in range(8):
                mm = t.matmul(PSA[:, 1024 + 16 * tt: 1024 + 16 * tt + 16],
                              LGT[:, 128 * tt: 128 * tt + 128],
                              IDF[0:16, 0:16], start=True, stop=True)
            crt.inc(mm)
        lt_done = crt.v

        @blk.vector
        def _(v, lt_done=lt_done):
            v.wait_ge(crt.h, lt_done)
            v.tensor_copy(LGtm[:, 0:128], PSA[:, 1024:1152])
            v.drain()
            for tt in range(8):
                v.reduce_max(SCL[:, tt: tt + 1], LGtm[:, 16 * tt: 16 * tt + 16], axis=AX.X)
                v.drain()
            crte.inc(v.tensor_scalar(GTMP[:, 0:8], SCL[:, 0:8], -1.0, None,
                                     op0=AluOpType.mult))
        negmax_done = crte.v

        @blk.scalar
        def _(sc, negmax_done=negmax_done):
            sc.wait_ge(crte.h, negmax_done)
            for tt in range(8):
                act = sc.activation(ETM8[:, 16 * tt: 16 * tt + 16],
                                    LGtm[:, 16 * tt: 16 * tt + 16],
                                    AF.Exp, bias=GTMP[:, tt: tt + 1])
            csq.inc(act)
        exp8_done = csq.v

        @blk.vector
        def _(v, exp8_done=exp8_done):
            v.wait_ge(csq.h, exp8_done)
            for tt in range(8):
                E8 = ETM8[:, 16 * tt: 16 * tt + 16]
                v.drain()
                v.reduce_sum(SCL[:, 2:3], E8, axis=AX.X)
                v.drain()
                v.reciprocal(SCL[:, 3:4], SCL[:, 2:3])
                v.drain()
                v.tensor_scalar(PTMP[:, :], E8, SCL[:, 3:4], None, op0=AluOpType.mult)
                v.drain()
                v.reduce_max(SCL[:, 4:5], PTMP[:, :], axis=AX.X)
                v.drain()
                v.tensor_scalar(GTMP[:, 8:24], PTMP[:, :], SCL[:, 4:5], None, op0=AluOpType.is_ge)
                v.drain()
                v.tensor_tensor(ETMP[:, :], PTMP[:, :], GTMP[:, 8:24], op=AluOpType.subtract)
                v.drain()
                v.reduce_max(SCL[:, 5:6], ETMP[:, :], axis=AX.X)
                v.drain()
                v.tensor_tensor(SCL[:, 6:7], SCL[:, 4:5], SCL[:, 5:6], op=AluOpType.add)
                v.drain()
                v.tensor_scalar(SCL[:, 6:7], SCL[:, 6:7], 1e-6, None, op0=AluOpType.add)
                v.drain()
                v.reciprocal(SCL[:, 7:8], SCL[:, 6:7])
                v.drain()
                v.tensor_scalar(SCL[:, 7:8], SCL[:, 7:8], 0.5, None, op0=AluOpType.mult)
                v.drain()
                v.tensor_scalar(GTMP[:, 8:24], PTMP[:, :], SCL[:, 5:6], None, op0=AluOpType.is_ge)
                v.drain()
                v.tensor_tensor(PTMP[:, :], PTMP[:, :], GTMP[:, 8:24], op=AluOpType.mult)
                v.drain()
                mm = v.tensor_scalar(CMBtm[:, 16 * tt: 16 * tt + 16], PTMP[:, :],
                                     SCL[:, 7:8], None, op0=AluOpType.mult)
            crte.inc(mm)
        cmb_done = crte.v

        @blk.tensor
        def _(t, cmb_done=cmb_done):
            t.wait_ge(crte.h, cmb_done)
            for tt in range(8):
                mm = t.matmul(PSA[0:16, 128 * tt: 128 * tt + 128],
                              CMBtm[:, 16 * tt: 16 * tt + 16],
                              IDF[:, :], start=True, stop=True)
            crt.inc(mm)
        cmbT_done = crt.v

        @blk.vector
        def _(v, cmbT_done=cmbT_done):
            v.wait_ge(crt.h, cmbT_done)
            crte.inc(v.tensor_copy(CMB[:, :], PSA[0:16, 0:1024]))
        router_done = crte.v

        for j in range(8):
            @blk.tensor
            def _(t, j=j, router_done=router_done):
                t.wait_ge(crte.h, router_done)
                t.wait_ge(ccon.h, m_ready)
                if j >= 1:
                    t.wait_ge(ccte.h, j)
                for half in range(2):
                    t.matmul(PSA[:, 1024 + 512 * half: 1024 + 512 * half + 512],
                             M1S[:, 128 * j: 128 * j + 128],
                             CMB[:, 512 * half: 512 * half + 512],
                             start=True, stop=True)
                    mm = t.matmul(PSA[:, 512 * half: 512 * half + 512],
                                  M2S[:, :],
                                  RH[:, 512 * half: 512 * half + 512],
                                  start=True, stop=True)
                cct.inc(mm)
            myj = cct.v

            @blk.vector
            def _(v, j=j, myj=myj):
                v.wait_ge(cct.h, myj)
                for half in range(2):
                    v.tensor_copy(M2SB[:, :], PSA[:, 1024 + 512 * half: 1024 + 512 * half + 512])
                    mm = v.tensor_tensor(
                        CT[:, 1024 * j + 512 * half: 1024 * j + 512 * half + 512],
                        M2SB[:, :], PSA[:, 512 * half: 512 * half + 512],
                        op=AluOpType.mult)
                ccte.inc(mm)
        ct_done = ccte.v
        if dump:
            @blk.sync
            def _(s_):
                s_.wait_ge(ccte.h, ct_done)
                cosp.inc(s_.dma_start(DBG5, CT[:, :]), 16)
                cosp.inc(s_.dma_start(DBG6, RH[:, :]), 16)
                cosp.inc(s_.dma_start(DBG7, LGT[:, :]), 16)
                cosp.inc(s_.dma_start(DBG8, CMB[:, :]), 16)

        ct_base = ct_done - 8
        for rnd12, nslot in ((1, n_ffb0), (2, n_ffb1)):
            @blk.tensor
            def _(t, n=nslot, b_off=512 * (rnd12 - 1), ct_base=ct_base):
                for j in range(8):
                    t.wait_ge(ccte.h, ct_base + j + 1)
                    mm = t.matmul(PSB[:, b_off: b_off + 512],
                                  wslot(n, 5120)[:, 4096 + 128 * j: 4096 + 128 * j + 128],
                                  CT[:, 1024 * j: 1024 * j + 512],
                                  start=False, stop=(j == 7))
                cffn.inc(mm)

        ffn_rounds = 0

        def emit_ffn_epilogue(ot, tc, rnd):
            rr = (rnd - 1) % 2

            myffn = cffn.v

            @blk.vector
            def _(v, ot=ot, tc=tc, b_off=512 * rr, myffn=myffn):
                v.wait_ge(cffn.h, myffn)
                v.wait_ge(csyc.h, bb_ready)
                mm = v.scalar_tensor_tensor(
                    REG1[:, 1024 * ot + 512 * tc: 1024 * ot + 512 * tc + 512],
                    PSB[:, b_off: b_off + 512],
                    BBS[:, ot: ot + 1],
                    REG2[:, 1024 * ot + 512 * tc: 1024 * ot + 512 * tc + 512],
                    op0=AluOpType.add, op1=AluOpType.add)
                ch2.inc(mm)

            @blk.vector
            def _(v, ot=ot, tc=tc, rnd=rnd, rr=rr):
                if rnd > 2:
                    v.wait_ge(cstm.h, 64 + rnd - 2)
                sl = REG1[:, 1024 * ot + 512 * tc: 1024 * ot + 512 * tc + 512]
                csqv.inc(v.tensor_tensor(
                    SQ[:, 512 * rr: 512 * rr + 512], sl, sl, op=AluOpType.mult))

            mysq = csqv.v

            def stats_item(ot=ot, tc=tc, mysq=mysq, rr=rr):
                @blk.tensor
                def _(t):
                    t.wait_ge(csqv.h, mysq)
                    t.matmul(PSA[:, st_sum[tc]: st_sum[tc] + 512], ONE4K[:, :],
                             REG1[:, 1024 * ot + 512 * tc: 1024 * ot + 512 * tc + 512],
                             start=(ot == 0), stop=(ot == 31))
                    mm = t.matmul(PSA[:, st_sq[tc]: st_sq[tc] + 512], ONE4K[:, :],
                                  SQ[:, 512 * rr: 512 * rr + 512],
                                  start=(ot == 0), stop=(ot == 31))
                    cstm.inc(mm)
            pending_stats.append(stats_item)

        out_n = [0]

        def emit_out(ot, tc, myfin):
            out_n[0] += 1
            n = out_n[0]
            oslot = (n - 1) % 4

            @blk.vector
            def _(v, ot=ot, tc=tc, myfin=myfin, oslot=oslot, n=n):
                v.wait_ge(cfin.h, myfin)
                if n > 4:
                    v.wait_ge(cout.h, 16 * (n - 4))
                osl = OST[:, 512 * oslot: 512 * oslot + 512]
                v.tensor_tensor(osl,
                                REG1[:, 1024 * ot + 512 * tc: 1024 * ot + 512 * tc + 512],
                                RSTD2[:, 512 * tc: 512 * tc + 512], op=AluOpType.mult)
                cx1.inc(v.tensor_tensor(
                    osl, osl, BT2[:, 512 * tc: 512 * tc + 512], op=AluOpType.subtract))

            myo = cx1.v

            @blk.sync
            def _(s_, ot=ot, tc=tc, oslot=oslot, myo=myo):
                s_.wait_ge(cx1.h, myo)
                cout.inc(s_.dma_start(
                    OUT[128 * ot: 128 * ot + 128, 512 * tc: 512 * tc + 512],
                    OST[:, 512 * oslot: 512 * oslot + 512]), 16)

        for rnd12 in (1, 2):
            ffn_rounds += 1
            emit_ffn_epilogue(rnd12 - 1, 0, rnd12)

        norm_q = []
        for sweep_tc in range(2):
            for ot in (range(2, 32) if sweep_tc == 0 else range(32)):
                n = nblk
                nblk += 1
                ffn_rounds += 1
                rnd = ffn_rounds
                wslot_guard.append((cffn.h, rnd))

                @blk.gpsimd
                def _(g, n=n, ot=ot):
                    gd = wslot_guard[n - 3]
                    g.wait_ge(gd[0], gd[1])
                    cw.inc(g.dma_start(wslot(n, 5120), FFB[ot]), 16)

                b_off = 512 * ((rnd - 1) % 2)

                @blk.tensor
                def _(t, n=n, tc=sweep_tc, b_off=b_off, rnd=rnd):
                    t.wait_ge(cw.h, 16 * (n + 1))
                    t.wait_ge(ch2.h, rnd - 2)
                    for ct in range(32):
                        t.matmul(PSB[:, b_off: b_off + 512],
                                 wslot(n, 5120)[:, 128 * ct: 128 * ct + 128],
                                 REG2[:, 1024 * ct + 512 * tc: 1024 * ct + 512 * tc + 512],
                                 start=(ct == 0), stop=False)
                    for j in range(8):
                        mm = t.matmul(PSB[:, b_off: b_off + 512],
                                      wslot(n, 5120)[:, 4096 + 128 * j: 4096 + 128 * j + 128],
                                      CT[:, 1024 * j + 512 * tc: 1024 * j + 512 * tc + 512],
                                      start=False, stop=(j == 7))
                    cffn.inc(mm)
                flush_stats()
                emit_ffn_epilogue(ot, sweep_tc, rnd)
                for _i in range(2):
                    if norm_q:
                        norm_q.pop(0)()

            flush_stats(0)
            emit_ln_finalize(sweep_tc, cstm.v)
            myfin = cfin.v
            for oto in range(32):
                norm_q.append(lambda oto=oto, tc=sweep_tc, myfin=myfin:
                              emit_out(oto, tc, myfin))

        while norm_q:
            norm_q.pop(0)()

        if dump:
            @blk.sync
            def _(s_):
                s_.wait_ge(ch2.h, ch2.v)
                cosp.inc(s_.dma_start(DBG9, REG1[:, 0:32768]), 16)
                cosp.inc(s_.dma_start(RD1, RDB1[:, :]), 16)
                cosp.inc(s_.dma_start(RD2, RDB2[:, :]), 16)
                cosp.inc(s_.dma_start(RD3, RDB3[:, :]), 16)

        @blk.sync
        def _(s_):
            s_.wait_ge(cout.h, cout.v)
            s_.wait_ge(coutg.h, coutg.v)

        p4.close()
        p3.close()

    return nc


def _host_prep(inputs):
    f32 = np.float32
    x = np.ascontiguousarray(np.asarray(inputs["L0_attn_x"], f32))
    wq = np.asarray(inputs["L0_attn_wq"], f32)
    wk = np.asarray(inputs["L0_attn_wk"], f32)
    wv = np.asarray(inputs["L0_attn_wv"], f32)
    wo = np.asarray(inputs["L0_attn_wo"], f32)
    bw = np.asarray(inputs["L0_ffn_base_w"], f32)
    bb = np.asarray(inputs["L0_ffn_base_b"], f32)
    A = np.asarray(inputs["L0_ffn_A"], f32)
    Bm = np.asarray(inputs["L0_ffn_B"], f32)
    rw = np.asarray(inputs["L0_ffn_router_w"], f32)

    WQ_h = np.ascontiguousarray(
        wq.reshape(32, 128, 32, 128).transpose(2, 1, 0, 3)).reshape(32, 128, 4096)
    WK_h = np.ascontiguousarray(
        wk.reshape(32, 128, 8, 128).transpose(2, 1, 0, 3)).reshape(8, 128, 4096)
    WV_h = np.ascontiguousarray(
        wv.reshape(32, 128, 8, 128).transpose(2, 1, 0, 3)).reshape(8, 128, 4096)
    WKV_h = np.ascontiguousarray(np.concatenate([WK_h, WV_h], 0))
    WO_h = np.ascontiguousarray(
        wo.reshape(32, 128, 32, 128).transpose(2, 1, 0, 3)).reshape(32, 128, 4096)
    BWT_h = np.ascontiguousarray(
        bw.reshape(32, 128, 32, 128).transpose(0, 3, 2, 1)).reshape(32, 128, 4096)
    BF_h = np.ascontiguousarray(
        Bm.transpose(0, 2, 1).reshape(8, 128, 32, 128).transpose(2, 1, 0, 3)
    ).reshape(32, 128, 1024)
    FFB_h = np.ascontiguousarray(np.concatenate([BWT_h, BF_h], axis=2))
    art = np.concatenate([A.T, rw.T], axis=1)
    ART_h = np.ascontiguousarray(
        art.reshape(32, 128, 80).transpose(1, 0, 2)).reshape(128, 2560)

    M1_h = np.zeros((16, 1024), f32)
    pp = np.arange(1024)
    M1_h[2 * (pp // 128) + (pp % 128) // 64, pp] = 1.0
    M2_h = np.zeros((64, 128), f32)
    M2_h[np.arange(128) % 64, np.arange(128)] = 1.0
    BB_h = np.ascontiguousarray(bb.reshape(32, 128).T)
    ID_h = np.eye(128, dtype=f32)

    bf = lambda a: np.ascontiguousarray(a.astype(BF16_NP))
    shared = dict(WKV=bf(WKV_h), WQ=bf(WQ_h), WOD=bf(WO_h), FFB=bf(FFB_h),
                  ARTW=bf(ART_h), M1D=M1_h, M2D=M2_h, BBD=BB_h, IDD=ID_h)

    in_maps = []
    meta = []
    for b in range(4):
        xT = np.ascontiguousarray(x[b].T)
        XT_h = bf(np.ascontiguousarray(
            xT.reshape(32, 128, 4, 512).transpose(2, 1, 0, 3)))
        for role in range(2):
            ch = CHUNKS[role]
            cols = np.concatenate([np.arange(c * 256, (c + 1) * 256) for c in ch])
            xown = xT[:, cols]
            XOWN_h = np.ascontiguousarray(
                xown.reshape(32, 128, 1024).transpose(1, 0, 2))
            m4 = np.zeros((4, 4, 128, 256), f32)
            for s in range(4):
                q0 = ch[s] * 256
                for j in range(4):
                    kv0 = (4 * s + j) * 128
                    kvi = kv0 + np.arange(128)[:, None]
                    qi = q0 + np.arange(256)[None, :]
                    m4[s, j] = (kvi <= qi).astype(f32)
            MSK_h = np.ascontiguousarray(m4.transpose(2, 0, 1, 3)).reshape(128, 4096)
            im = dict(shared)
            im["XT"] = XT_h
            im["XOWN"] = XOWN_h
            im["MSK"] = bf(MSK_h)
            in_maps.append(im)
            meta.append((b, ch))
    return in_maps, meta


def kernel(**inputs):
    in_maps, meta = _host_prep(inputs)
    nc = build_program()
    res = run_bass_kernel_spmd(nc, in_maps, core_ids=list(range(8)))
    out = np.empty((4, 2048, 4096), np.float32)
    for i, r in enumerate(res.results):
        b, ch = meta[i]
        oT = np.asarray(r["OUT"])
        o = np.ascontiguousarray(oT.T)
        for k, c in enumerate(ch):
            out[b, c * 256:(c + 1) * 256] = o[k * 256:(k + 1) * 256]
    return out


# version 25 (final)
# speedup vs baseline: 1.0831x; 1.0104x over previous
import numpy as np
import ml_dtypes
from contextlib import ExitStack

BF16_NP = ml_dtypes.bfloat16

import concourse.bass as bass
import concourse.mybir as mybir
from concourse.bass_utils import run_bass_kernel_spmd
from concourse.alu_op_type import AluOpType

F32 = mybir.dt.float32
BF16 = mybir.dt.bfloat16
AF = mybir.ActivationFunctionType
AX = mybir.AxisListType

NH, NKV, HD, D = 32, 8, 128, 4096
SCALE = float(HD) ** -0.5
EPS = 1e-5
CHUNKS = {0: [0, 3, 4, 7], 1: [1, 2, 5, 6]}
INV_D = 1.0 / 4096.0


class Ctr:
    def __init__(self, stack, nc, name):
        self.h = stack.enter_context(nc.semaphore(name))
        self.v = 0

    def inc(self, inst, k=1):
        inst.then_inc(self.h, k)
        self.v += k
        return self.v


def build_program(phases=4, nheads=32, do_av=True, do_norm=True, do_tr=True, do_spill=True, do_exp=True, do_mask=True, dump=False):
    nc = bass.Bass("TRN2", target_bir_lowering=False)

    XT = nc.dram_tensor("XT", [4, 128, 32, 512], BF16, kind="ExternalInput").ap()
    XOWN = nc.dram_tensor("XOWN", [128, 32, 1024], F32, kind="ExternalInput").ap()
    WKV = nc.dram_tensor("WKV", [16, 128, 4096], BF16, kind="ExternalInput").ap()
    WQ = nc.dram_tensor("WQ", [32, 128, 4096], BF16, kind="ExternalInput").ap()
    WOD = nc.dram_tensor("WOD", [32, 128, 4096], BF16, kind="ExternalInput").ap()
    FFB = nc.dram_tensor("FFB", [32, 128, 5120], BF16, kind="ExternalInput").ap()
    ARTW = nc.dram_tensor("ARTW", [128, 2560], BF16, kind="ExternalInput").ap()
    MSK = nc.dram_tensor("MSK", [128, 4096], BF16, kind="ExternalInput").ap()
    M1D = nc.dram_tensor("M1D", [16, 1024], F32, kind="ExternalInput").ap()
    M2D = nc.dram_tensor("M2D", [64, 128], F32, kind="ExternalInput").ap()
    BBD = nc.dram_tensor("BBD", [128, 32], F32, kind="ExternalInput").ap()
    IDD = nc.dram_tensor("IDD", [128, 128], F32, kind="ExternalInput").ap()
    OUT = nc.dram_tensor("OUT", [4096, 1024], F32, kind="ExternalOutput").ap()
    OSC = nc.dram_tensor("OSC", [128, 32768], BF16, kind="Internal").ap()
    if dump:
        DBG1 = nc.dram_tensor("DBG1", [128, 32896], BF16, kind="ExternalOutput").ap()
        DBG2 = nc.dram_tensor("DBG2", [128, 32768], BF16, kind="ExternalOutput").ap()
        DBG3 = nc.dram_tensor("DBG3", [128, 32768], BF16, kind="ExternalOutput").ap()
        DBG4 = nc.dram_tensor("DBG4", [128, 32768], BF16, kind="ExternalOutput").ap()
        DBG5 = nc.dram_tensor("DBG5", [128, 8192], BF16, kind="ExternalOutput").ap()
        DBG6 = nc.dram_tensor("DBG6", [64, 1024], BF16, kind="ExternalOutput").ap()
        DBG7 = nc.dram_tensor("DBG7", [16, 1024], F32, kind="ExternalOutput").ap()
        DBG8 = nc.dram_tensor("DBG8", [16, 1024], BF16, kind="ExternalOutput").ap()
        DBG9 = nc.dram_tensor("DBG9", [128, 32768], BF16, kind="ExternalOutput").ap()
        RD1 = nc.dram_tensor("RD1", [128, 128], F32, kind="ExternalOutput").ap()
        RD2 = nc.dram_tensor("RD2", [128, 128], F32, kind="ExternalOutput").ap()
        RD3 = nc.dram_tensor("RD3", [128, 64], F32, kind="ExternalOutput").ap()

    stack = ExitStack()
    with stack:
        E = stack.enter_context

        REG1 = E(nc.sbuf_tensor("REG1", [128, 32896], BF16))
        REG2 = E(nc.sbuf_tensor("REG2", [128, 32768], BF16))
        WST = E(nc.sbuf_tensor("WST", [128, 3 * 5120], BF16))
        IDB = E(nc.sbuf_tensor("IDB", [128, 128], BF16))
        IDF = E(nc.sbuf_tensor("IDF", [128, 128], F32))
        ONE4K = E(nc.sbuf_tensor("ONE4K", [128, 128], BF16))
        EPSB = E(nc.sbuf_tensor("EPSB", [128, 1], F32))

        KT = lambda g, a, b: REG1[:, 2048 * g + a: 2048 * g + b]
        VA = lambda kt, g: REG1[:, 16384 + (kt * 8 + g) * 129: 16384 + (kt * 8 + g) * 129 + 129]
        VAall = REG1[:, 16384:32896]

        def wslot(n, w=5120):
            i = n % 3
            return WST[:, 5120 * i: 5120 * i + w]

        p2 = ExitStack()
        MASKS = p2.enter_context(nc.sbuf_tensor("MASKS", [128, 4096], BF16))
        QT = p2.enter_context(nc.sbuf_tensor("QT", [128, 2 * 1024], BF16))
        UB = p2.enter_context(nc.sbuf_tensor("UB", [128, 2 * 1024], BF16))
        OTH = p2.enter_context(nc.sbuf_tensor("OTH", [128, 2 * 1024], BF16))
        OPR = p2.enter_context(nc.sbuf_tensor("OPR", [128, 2 * 256], BF16))
        RDN = p2.enter_context(nc.sbuf_tensor("RDN", [128, 4], F32))
        p1 = ExitStack()
        VTS = p1.enter_context(nc.sbuf_tensor("VTS", [128, 2 * 1024], BF16))

        PSA = E(nc.psum_tensor("PSA", [128, 2048], F32))
        PSB = E(nc.psum_tensor("PSB", [128, 1024], F32))
        PST1T = E(nc.psum_tensor("PST1", [128, 1024], BF16))
        PST2T = E(nc.psum_tensor("PST2", [128, 1024], BF16))
        PST1 = PST1T[:, 0:128]
        PST2 = PST2T[:, 0:128]

        ccon = Ctr(stack, nc, "ccon")
        csyc = Ctr(stack, nc, "csyc")
        cw = Ctr(stack, nc, "cw")
        cwr = Ctr(stack, nc, "cwr")
        cx = Ctr(stack, nc, "cx")
        cxr = Ctr(stack, nc, "cxr")
        ckv = Ctr(stack, nc, "ckv")
        ckve = Ctr(stack, nc, "ckve")
        ctr = Ctr(stack, nc, "ctr")
        ctre = Ctr(stack, nc, "ctre")
        cpk = Ctr(stack, nc, "cpk")
        cpkr = Ctr(stack, nc, "cpkr")
        cqf = Ctr(stack, nc, "cqf")
        cexp = Ctr(stack, nc, "cexp")
        cmsk = Ctr(stack, nc, "cmsk")
        cur = Ctr(stack, nc, "cur")
        cav = Ctr(stack, nc, "cav")
        cnrm = Ctr(stack, nc, "cnrm")
        cosp = Ctr(stack, nc, "cosp")
        cxf = Ctr(stack, nc, "cxf")
        cwo = Ctr(stack, nc, "cwo")
        chh = Ctr(stack, nc, "chh")
        cxo = Ctr(stack, nc, "cxo")
        cxor = Ctr(stack, nc, "cxor")
        csq = Ctr(stack, nc, "csq")
        cst = Ctr(stack, nc, "cst")
        cstm = Ctr(stack, nc, "cstm")
        cfin = Ctr(stack, nc, "cfin")
        cx1 = Ctr(stack, nc, "cx1")
        crld = Ctr(stack, nc, "crld")
        cart = Ctr(stack, nc, "cart")
        crt = Ctr(stack, nc, "crt")
        crte = Ctr(stack, nc, "crte")
        cct = Ctr(stack, nc, "cct")
        ccte = Ctr(stack, nc, "ccte")
        cffn = Ctr(stack, nc, "cffn")
        cffb = Ctr(stack, nc, "cffb")
        cpkq = Ctr(stack, nc, "cpkq")
        cx1g = Ctr(stack, nc, "cx1g")
        csqv = Ctr(stack, nc, "csqv")
        ch2 = Ctr(stack, nc, "ch2")
        cout = Ctr(stack, nc, "cout")
        coutg = Ctr(stack, nc, "coutg")

        blk = E(nc.Block())

        def emit_constants():
            @blk.gpsimd
            def _(g):
                ccon.inc(g.dma_start(MASKS[:, :], MSK), 16)
                ccon.inc(g.dma_start(IDB[:, :], IDD), 16)
                g.memset(VAall, 1.0)
                g.memset(EPSB[:, :], EPS)
                ccon.inc(g.memset(ONE4K[:, :], INV_D), 1)

        @blk.sync
        def _(s_):
            csyc.inc(s_.dma_start(IDF[:, :], IDD), 16)

        csyc_idf = csyc.v

        def xstage(cc):
            i = cc % 2
            return REG2[:, 16384 * i: 16384 * i + 16384]

        nblk = 0
        wslot_guard = []
        kv_evt = 0
        vts_free_at = [0, 0]
        pst_free_at = [0, 0]
        tr_idx = 0

        for cc in range(4):
            @blk.gpsimd
            def _(g, cc=cc):
                if cc >= 2:
                    g.wait_ge(ckv.h, 16 * (cc - 1))
                cx.inc(g.dma_start(xstage(cc), XT[cc]), 16)

            for gg in range(16):
                n = nblk
                nblk += 1

                @blk.gpsimd
                def _(g, n=n, gg=gg):
                    if n >= 3:
                        gd = wslot_guard[n - 3]
                        g.wait_ge(gd[0], gd[1])
                    cw.inc(g.dma_start(wslot(n, 4096), WKV[gg]), 16)

                if cc == 0 and gg == 0:
                    emit_constants()
                    ccon_ready = ccon.v

                ev = kv_evt
                kv_evt += 1
                r = ev % 4
                wslot_guard.append((ckv.h, ev + 1))

                @blk.tensor
                def _(t, n=n, ci=cc, ev=ev, r=r, gg=gg):
                    t.wait_ge(cw.h, 16 * (n + 1))
                    t.wait_ge(cx.h, 16 * (ci + 1))
                    if ev >= 4:
                        t.wait_ge(ckve.h, ev - 3)
                    for ct in range(32):
                        mm = t.matmul(
                            PSA[:, 512 * r: 512 * r + 512],
                            wslot(n, 4096)[:, 128 * ct: 128 * ct + 128],
                            xstage(cc)[:, 512 * ct: 512 * ct + 512],
                            start=(ct == 0), stop=(ct == 31),
                        )
                    ckv.inc(mm)

                if gg < 8:
                    @blk.vector
                    def _(v, ev=ev, r=r, g_=gg, cc=cc):
                        v.wait_ge(ckv.h, ev + 1)
                        ckve.inc(v.tensor_copy(
                            KT(g_, 512 * cc, 512 * cc + 512),
                            PSA[:, 512 * r: 512 * r + 512]))
                else:
                    vslot = ev % 2
                    need_tr = vts_free_at[vslot]

                    @blk.vector
                    def _(v, ev=ev, r=r, vslot=vslot, need_tr=need_tr):
                        v.wait_ge(ckv.h, ev + 1)
                        if need_tr:
                            v.wait_ge(ctr.h, need_tr)
                        ckve.inc(v.tensor_copy(
                            VTS[:, 1024 * vslot: 1024 * vslot + 512],
                            PSA[:, 512 * r: 512 * r + 512]))

                    my_evac = ckve.v
                    for tsub in range(4):
                        ti = tr_idx
                        tr_idx += 1
                        tp = ti % 2
                        PSTt = PST1 if tp == 0 else PST2
                        need_ev = pst_free_at[tp]

                        @blk.tensor
                        def _(t, tp=tp, PSTt=PSTt, vslot=vslot, ti=ti,
                              tsub=tsub, need_ev=need_ev, my_evac=my_evac):
                            t.wait_ge(ckve.h, my_evac)
                            if ti == 0:
                                t.wait_ge(ccon.h, 32)
                            if need_ev:
                                t.wait_ge(ctre.h, need_ev)
                            ctr.inc(t.transpose(
                                PSTt,
                                VTS[:, 1024 * vslot + 128 * tsub: 1024 * vslot + 128 * tsub + 128],
                                IDB[:, :]))

                        kt_g = 4 * cc + tsub
                        g_ = gg - 8

                        @blk.vector
                        def _(v, ti=ti, PSTt=PSTt, kt_g=kt_g, g_=g_):
                            v.wait_ge(ctr.h, ti + 1)
                            if ti == 0:
                                v.wait_ge(ccon.h, ccon_ready)
                            ctre.inc(v.tensor_copy(
                                VA(kt_g, g_)[:, 0:128], PSTt))

                        pst_free_at[tp] = ctre.v
                    vts_free_at[vslot] = ctr.v

        @blk.gpsimd
        def _(g):
            g.wait_ge(ckv.h, 48)
            cx.inc(g.dma_start(REG2[:, 0:16384], XOWN[:, 0:16, :]), 16)
        xown_r0 = cx.v

        @blk.gpsimd
        def _(g):
            g.wait_ge(ckv.h, ckv.v)
            cx.inc(g.dma_start(REG2[:, 16384:24576], XOWN[:, 16:24, :]), 16)
        xown_r1 = cx.v

        @blk.gpsimd
        def _(g):
            cx.inc(g.dma_start(REG2[:, 24576:32768], XOWN[:, 24:32, :]), 16)
        xown_r2 = cx.v

        p1.close()
        if phases < 2:
            p2.close()
            return nc

        pk = 0
        u_idx = 0
        av_idx = 0
        m_idx = 0
        kvdone = ckve.v
        va_done = ctre.v
        if dump:
            @blk.sync
            def _(s_):
                s_.wait_ge(ckve.h, kvdone)
                s_.wait_ge(ctre.h, va_done)
                cosp.inc(s_.dma_start(DBG1, REG1[:, :]), 16)

        q31_fill = [0]
        head_lastS = {}
        pack_rel = []

        def pack_off(p):
            return 1024 * (p % 2)

        qt_slice = lambda h, a, b: QT[:, 1024 * (h % 2) + a: 1024 * (h % 2) + b]

        def emit_qproj(h):
            nonlocal pk, nblk
            n = nblk
            nblk += 1

            @blk.gpsimd
            def _(g, n=n, h=h):
                gd = wslot_guard[n - 3]
                g.wait_ge(gd[0], gd[1])
                cw.inc(g.dma_start(wslot(n, 4096), WQ[h]), 16)

            mypk = pk
            pk += 1
            off = pack_off(mypk)
            wslot_guard.append((cpk.h, mypk + 1))

            @blk.tensor
            def _(t, n=n, mypk=mypk, off=off, h=h):
                t.wait_ge(cw.h, 16 * (n + 1))
                if mypk >= 2:
                    gd = pack_rel[mypk - 2]
                    t.wait_ge(gd[0], gd[1])
                for half in range(2):
                    for ct in range(32):
                        if h == 0 and half == 0:
                            if ct == 0:
                                t.wait_ge(cx.h, xown_r0)
                            elif ct == 16:
                                t.wait_ge(cx.h, xown_r1)
                            elif ct == 24:
                                t.wait_ge(cx.h, xown_r2)
                        mm = t.matmul(
                            PSA[:, off + 512 * half: off + 512 * half + 512],
                            wslot(n, 4096)[:, 128 * ct: 128 * ct + 128],
                            REG2[:, 1024 * ct + 512 * half: 1024 * ct + 512 * half + 512],
                            start=(ct == 0), stop=(ct == 31),
                        )
                cpk.inc(mm)
                if h == 31:
                    q31_fill[0] = cpk.v

            @blk.vector
            def _(v, mypk=mypk, off=off, h=h):
                v.wait_ge(cpk.h, mypk + 1)
                if h >= 2:
                    v.wait_ge(cpk.h, head_lastS[h - 2])
                v.tensor_copy(qt_slice(h, 0, 512), PSA[:, off: off + 512])
                cpkq.inc(v.tensor_copy(qt_slice(h, 512, 1024), PSA[:, off + 512: off + 1024]))
            pack_rel.append((cpkq.h, cpkq.v))

        from collections import deque
        workq = deque()
        last_oth = []
        exp_jobs = []
        tr_base = ctr.v
        tr_created = 0
        av_created = 0
        round_tr_last = {}
        pst_guard = [0, 0]
        ctre_assign = [ctre.v]

        def flush(n_keep=2):
            while len(workq) > n_keep:
                workq.popleft()()

        def emit_attn(h):
            nonlocal pk, u_idx, av_idx, m_idx, tr_created, av_created
            g_ = h // 4
            for s in range(4):
                myav = av_idx
                av_idx += 1
                a_off = 512 * (myav % 2)
                npacks = s + 1
                for p in range(npacks):
                    mypk = pk
                    pk += 1
                    myu = u_idx
                    u_idx += 1
                    off = pack_off(mypk)
                    masked = (p == s)

                    @blk.tensor
                    def _(t, mypk=mypk, off=off, s=s, p=p, h=h, g_=g_):
                        if mypk >= 2:
                            gd = pack_rel[mypk - 2]
                            t.wait_ge(gd[0], gd[1])
                        if h == 0 and s == 0 and p == 0:
                            t.wait_ge(ckve.h, kvdone)
                        for j in range(4):
                            kt = 4 * p + j
                            mm = t.matmul(
                                PSA[:, off + 256 * j: off + 256 * j + 256],
                                KT(g_, 128 * kt, 128 * kt + 128),
                                qt_slice(h, 256 * s, 256 * s + 256),
                                start=True, stop=True,
                            )
                        cpk.inc(mm)

                    exp_jobs.append((mypk, myu, off))
                    pack_rel.append((cpkr.h, len(exp_jobs)))
                    gd_pack = pack_rel[-1]

                    if masked:
                        m_idx += 1
                        mym = m_idx

                        @blk.vector
                        def _(v, gd=gd_pack, myu=myu, s=s):
                            v.wait_ge(gd[0], gd[1])
                            u = UB[:, 1024 * (myu % 2): 1024 * (myu % 2) + 1024]
                            cmsk.inc(v.tensor_tensor(
                                u, u, MASKS[:, 1024 * s: 1024 * s + 1024],
                                op=AluOpType.mult))
                    else:
                        mym = m_idx

                    av_created += 1
                    my_cur_val = av_created
                    first_av = (p == 0)

                    def av_item(myu=myu, masked=masked, mym=mym, p=p, s=s,
                                g_=g_, a_off=a_off, first_av=first_av,
                                myav=myav, gd=gd_pack):
                        @blk.tensor
                        def _(t):
                            if masked:
                                t.wait_ge(cmsk.h, mym)
                            else:
                                t.wait_ge(gd[0], gd[1])
                            if first_av:
                                if myav == 0:
                                    t.wait_ge(ctre.h, va_done)
                                if myav >= 2:
                                    t.wait_ge(cnrm.h, myav - 1)
                            u0 = 1024 * (myu % 2)
                            first = first_av
                            for j in range(4):
                                kt = 4 * p + j
                                for qs in range(2):
                                    mm = t.matmul(
                                        PSB[:, a_off + 129 * qs: a_off + 129 * qs + 129],
                                        UB[:, u0 + 256 * j + 128 * qs: u0 + 256 * j + 128 * qs + 128],
                                        VA(kt, g_),
                                        start=first, stop=(p == s and j == 3 and qs == 1),
                                    )
                                    first = False
                            cur.inc(mm)
                    workq.append(av_item)
                    flush()

                opr_gd = round_tr_last.get(myav - 2, 0)

                @blk.vector
                def _(v, myav=myav, a_off=a_off, avval=my_cur_val, opr_gd=opr_gd):
                    v.wait_ge(cur.h, avval)
                    if opr_gd:
                        v.wait_ge(ctr.h, opr_gd)
                    o = 256 * (myav % 2)
                    v.reciprocal(RDN[:, 0:1], PSB[:, a_off + 128: a_off + 129])
                    v.reciprocal(RDN[:, 1:2], PSB[:, a_off + 257: a_off + 258])
                    v.drain()
                    v.tensor_scalar(OPR[:, o: o + 128], PSB[:, a_off: a_off + 128],
                                    RDN[:, 0:1], None, op0=AluOpType.mult)
                    cnrm.inc(v.tensor_scalar(
                        OPR[:, o + 128: o + 256], PSB[:, a_off + 129: a_off + 257],
                        RDN[:, 1:2], None, op0=AluOpType.mult))
                mynrm = cnrm.v

                for qs in range(2):
                    ti = tr_created
                    tr_created += 1
                    tp = ti % 2
                    PSTt = PST1 if tp == 0 else PST2
                    need_ev = pst_guard[tp]
                    ctre_assign[0] += 1
                    my_ctre = ctre_assign[0]
                    pst_guard[tp] = my_ctre

                    def tr_item(myav=myav, qs=qs, PSTt=PSTt, need_ev=need_ev,
                                mynrm=mynrm, ti=ti, h=h, s=s):
                        @blk.tensor
                        def _(t):
                            t.wait_ge(cnrm.h, mynrm)
                            if need_ev:
                                t.wait_ge(ctre.h, need_ev)
                            o = 256 * (myav % 2)
                            ctr.inc(t.transpose(
                                PSTt, OPR[:, o + 128 * qs: o + 128 * qs + 128],
                                IDB[:, :]))

                        @blk.vector
                        def _(v):
                            v.wait_ge(ctr.h, tr_base + ti + 1)
                            if h >= 2 and s == 0 and qs == 0:
                                v.wait_ge(cosp.h, 16 * (h - 1))
                            ctre.inc(v.tensor_copy(
                                OTH[:, 1024 * (h % 2) + 256 * s + 128 * qs:
                                     1024 * (h % 2) + 256 * s + 128 * qs + 128],
                                PSTt))
                    workq.append(tr_item)
                    flush()
                round_tr_last[myav] = tr_base + tr_created

            head_lastS[h] = cpk.v

            myctre = ctre_assign[0]

            if h < nheads - 2:
                @blk.sync
                def _(s_, h=h, myctre=myctre):
                    s_.wait_ge(ctre.h, myctre)
                    cosp.inc(s_.dma_start(
                        OSC[:, 1024 * h: 1024 * h + 1024],
                        OTH[:, 1024 * (h % 2): 1024 * (h % 2) + 1024]), 16)
            else:
                last_oth.append((h, myctre))

        emit_qproj(0)
        emit_qproj(1)
        for h in range(nheads):
            emit_attn(h)
            if h + 2 < nheads:
                emit_qproj(h + 2)
        flush(0)

        @blk.scalar
        def _(sc):
            for i, (mypk, myu, off) in enumerate(exp_jobs):
                sc.wait_ge(cpk.h, mypk + 1)
                if myu >= 2:
                    sc.wait_ge(cur.h, myu - 1)
                act = sc.activation(
                    UB[:, 1024 * (myu % 2): 1024 * (myu % 2) + 1024],
                    PSA[:, off: off + 1024],
                    AF.Exp, scale=SCALE)
                act.then_inc(cpkr.h, 1)
        cpkr.v += len(exp_jobs)

        if nheads == 32:
            @blk.sync
            def _(s_):
                s_.wait_ge(cosp.h, cosp.v)
                s_.wait_ge(cpk.h, q31_fill[0])
                cosp.inc(s_.dma_start(REG2[:, 0:30720], OSC[:, 0:30720]), 16)

            for h, myctre in last_oth:
                @blk.vector
                def _(v, h=h, myctre=myctre):
                    v.wait_ge(ctre.h, myctre)
                    v.wait_ge(cpk.h, q31_fill[0])
                    crld.inc(v.tensor_copy(
                        REG2[:, 1024 * h: 1024 * h + 1024],
                        OTH[:, 1024 * (h % 2): 1024 * (h % 2) + 1024]))
        ot_ready = cosp.v
        oth_copied = crld.v
        if dump and nheads == 32:
            @blk.sync
            def _(s_):
                s_.wait_ge(cosp.h, ot_ready)
                cosp.inc(s_.dma_start(DBG2, REG2[:, :]), 16)
        ot_ready = cosp.v

        p2.close()
        if phases < 3:
            return nc

        p3 = ExitStack()
        SQ = p3.enter_context(nc.sbuf_tensor("SQ", [128, 2 * 512], BF16))
        SD = p3.enter_context(nc.sbuf_tensor("SD", [128, 512], F32))
        M2SB = p3.enter_context(nc.sbuf_tensor("M2SB", [128, 512], F32))
        RSTD2 = p3.enter_context(nc.sbuf_tensor("RSTD2", [128, 1024], BF16))
        BT2 = p3.enter_context(nc.sbuf_tensor("BT2", [128, 1024], BF16))
        p3a = ExitStack()
        XO = p3a.enter_context(nc.sbuf_tensor("XO", [128, 3 * 1024], F32))

        st_sum = [0, 1024]
        st_sq = [512, 1536]

        def emit_ln_finalize(tc, mystats, wide=True):
            sum_off, sq_off = st_sum[tc], st_sq[tc]
            RS_ = RSTD2[:, 512 * tc: 512 * tc + 512]
            BT_ = BT2[:, 512 * tc: 512 * tc + 512]

            @blk.vector
            def _(v, sum_off=sum_off, sq_off=sq_off, mystats=mystats):
                v.wait_ge(cstm.h, mystats)
                v.tensor_copy(M2SB[:, :], PSA[:, sum_off: sum_off + 512])
                v.drain()
                v.tensor_tensor(M2SB[:, :], M2SB[:, :], M2SB[:, :], op=AluOpType.mult)
                csq.inc(v.tensor_tensor(M2SB[:, :], PSA[:, sq_off: sq_off + 512],
                                        M2SB[:, :], op=AluOpType.subtract))
            myv = csq.v

            @blk.scalar
            def _(sc, myv=myv):
                sc.wait_ge(csq.h, myv)
                csq.inc(sc.activation(SD[:, :], M2SB[:, :], AF.Sqrt, bias=EPSB[:, 0:1]))
            mysd = csq.v

            @blk.vector
            def _(v, sum_off=sum_off, mysd=mysd, RS_=RS_, BT_=BT_):
                v.wait_ge(csq.h, mysd)
                with nc.allow_low_precision(reason="bf16 rstd, validated margin"):
                    v.reciprocal(RS_, SD[:, :])
                    v.drain()
                    cfin.inc(v.tensor_tensor(BT_, PSA[:, sum_off: sum_off + 512],
                                             RS_, op=AluOpType.mult))

        wo_rounds = 0
        pending_stats = []

        def flush_stats(keep=1):
            while len(pending_stats) > keep:
                pending_stats.pop(0)()

        for ot in range(32):
            n = nblk
            nblk += 1
            wslot_guard.append((cwo.h, 2 * (ot + 1)))

            @blk.gpsimd
            def _(g, n=n, ot=ot):
                gd = wslot_guard[n - 3]
                g.wait_ge(gd[0], gd[1])
                cw.inc(g.dma_start(wslot(n, 4096), WOD[ot]), 16)

            @blk.sync
            def _(s_, ot=ot):
                if ot >= 3:
                    s_.wait_ge(chh.h, 2 * ot - 4)
                elif ot == 0:
                    s_.wait_ge(crld.h, oth_copied)
                cxo.inc(s_.dma_start(XO[:, 1024 * (ot % 3): 1024 * (ot % 3) + 1024],
                                     XOWN[:, ot, :]), 16)

            for tc in range(2):
                wo_rounds += 1
                rnd = wo_rounds
                rr = (rnd - 1) % 2
                b_off = 512 * rr

                @blk.tensor
                def _(t, n=n, ot=ot, tc=tc, b_off=b_off, rnd=rnd):
                    t.wait_ge(cw.h, 16 * (n + 1))
                    if ot == 0 and tc == 0:
                        t.wait_ge(cosp.h, ot_ready)
                        t.wait_ge(crld.h, oth_copied)
                    if rnd > 2:
                        t.wait_ge(chh.h, rnd - 2)
                    for ht in range(32):
                        mm = t.matmul(
                            PSB[:, b_off: b_off + 512],
                            wslot(n, 4096)[:, 128 * ht: 128 * ht + 128],
                            REG2[:, 1024 * ht + 512 * tc: 1024 * ht + 512 * tc + 512],
                            start=(ht == 0), stop=(ht == 31),
                        )
                    cwo.inc(mm)
                flush_stats()

                mywo = cwo.v

                @blk.vector
                def _(v, ot=ot, tc=tc, b_off=b_off, mywo=mywo):
                    v.wait_ge(cwo.h, mywo)
                    v.wait_ge(cxo.h, 16 * (ot + 1))
                    mm = v.tensor_tensor(
                        REG1[:, 1024 * ot + 512 * tc: 1024 * ot + 512 * tc + 512],
                        PSB[:, b_off: b_off + 512],
                        XO[:, 1024 * (ot % 3) + 512 * tc: 1024 * (ot % 3) + 512 * tc + 512],
                        op=AluOpType.add)
                    chh.inc(mm)

                myhh = chh.v

                @blk.vector
                def _(v, ot=ot, tc=tc, rnd=rnd, rr=rr):
                    if rnd > 2:
                        v.wait_ge(cstm.h, rnd - 2)
                    sl = REG1[:, 1024 * ot + 512 * tc: 1024 * ot + 512 * tc + 512]
                    csqv.inc(v.tensor_tensor(
                        SQ[:, 512 * rr: 512 * rr + 512], sl, sl, op=AluOpType.mult))

                mysq = csqv.v

                def stats_item(ot=ot, tc=tc, mysq=mysq, rr=rr):
                    @blk.tensor
                    def _(t):
                        t.wait_ge(csqv.h, mysq)
                        t.matmul(PSA[:, st_sum[tc]: st_sum[tc] + 512], ONE4K[:, :],
                                 REG1[:, 1024 * ot + 512 * tc: 1024 * ot + 512 * tc + 512],
                                 start=(ot == 0), stop=(ot == 31))
                        mm = t.matmul(PSA[:, st_sq[tc]: st_sq[tc] + 512], ONE4K[:, :],
                                      SQ[:, 512 * rr: 512 * rr + 512],
                                      start=(ot == 0), stop=(ot == 31))
                        cstm.inc(mm)
                pending_stats.append(stats_item)
        flush_stats(0)

        emit_ln_finalize(0, cstm.v)
        fin0 = cfin.v
        emit_ln_finalize(1, cstm.v)
        fin1 = cfin.v
        x1v_base = cx1.v
        for tc in range(2):
            myfin = fin0 if tc == 0 else fin1
            for ot in range(32):
                @blk.vector
                def _(v, ot=ot, tc=tc, myfin=myfin):
                    v.wait_ge(cfin.h, myfin)
                    v.tensor_tensor(SQ[:, 512 * tc: 512 * tc + 512],
                                    REG1[:, 1024 * ot + 512 * tc: 1024 * ot + 512 * tc + 512],
                                    RSTD2[:, 512 * tc: 512 * tc + 512],
                                    op=AluOpType.mult)
                    cx1.inc(v.tensor_tensor(
                        REG2[:, 1024 * ot + 512 * tc: 1024 * ot + 512 * tc + 512],
                        SQ[:, 512 * tc: 512 * tc + 512], BT2[:, 512 * tc: 512 * tc + 512],
                        op=AluOpType.subtract))
        x1_done = cx1.v
        if dump:
            @blk.sync
            def _(s_):
                s_.wait_ge(cx1.h, x1_done)
                cosp.inc(s_.dma_start(DBG3, REG2[:, 0:32768]), 16)
                cosp.inc(s_.dma_start(DBG4, REG1[:, 0:32768]), 16)
        if phases < 4:
            p3a.close()
            p3.close()
            return nc

        p3a.close()
        p4 = ExitStack()
        CT = p4.enter_context(nc.sbuf_tensor("CT", [128, 8192], BF16))
        RH = p4.enter_context(nc.sbuf_tensor("RH", [64, 1024], BF16))
        LGT = p4.enter_context(nc.sbuf_tensor("LGT", [16, 1024], F32))
        LGtm = p4.enter_context(nc.sbuf_tensor("LGtm", [128, 128], F32))
        CMB = p4.enter_context(nc.sbuf_tensor("CMB", [16, 1024], BF16))
        M1S = p4.enter_context(nc.sbuf_tensor("M1S", [16, 1024], BF16))
        M2S = p4.enter_context(nc.sbuf_tensor("M2S", [64, 128], BF16))
        BBS = p4.enter_context(nc.sbuf_tensor("BBS", [128, 32], F32))
        ETMP = p4.enter_context(nc.sbuf_tensor("ETMP", [128, 16], F32))
        PTMP = p4.enter_context(nc.sbuf_tensor("PTMP", [128, 16], F32))
        GTMP = p4.enter_context(nc.sbuf_tensor("GTMP", [128, 24], F32))
        SCL = p4.enter_context(nc.sbuf_tensor("SCL", [128, 8], F32))
        ETM8 = p4.enter_context(nc.sbuf_tensor("ETM8", [128, 128], F32))
        PRB = p4.enter_context(nc.sbuf_tensor("PRB", [128, 128], F32))
        MSX = p4.enter_context(nc.sbuf_tensor("MSX", [128, 128], F32))
        SCL8 = p4.enter_context(nc.sbuf_tensor("SCL8", [128, 48], F32))
        CMBtm = LGtm
        RDB1 = p4.enter_context(nc.sbuf_tensor("RDB1", [128, 128], F32))
        RDB2 = p4.enter_context(nc.sbuf_tensor("RDB2", [128, 128], F32))
        RDB3 = p4.enter_context(nc.sbuf_tensor("RDB3", [128, 64], F32))
        OST = p4.enter_context(nc.sbuf_tensor("OST", [128, 4 * 512], F32))

        @blk.sync
        def _(s_):
            s_.wait_ge(chh.h, 64)
            csyc.inc(s_.dma_start(BBS[:, :], BBD), 16)
        bb_ready = csyc.v

        n_art = nblk
        nblk += 1
        wslot_guard.append((cart.h, 1))

        @blk.gpsimd
        def _(g, n=n_art):
            gd = wslot_guard[n - 3]
            g.wait_ge(gd[0], gd[1])
            cw.inc(g.dma_start(wslot(n, 2560), ARTW), 16)

        n_ffb0 = nblk
        nblk += 1
        wslot_guard.append((cffn.h, 1))

        @blk.gpsimd
        def _(g, n=n_ffb0):
            gd = wslot_guard[n - 3]
            g.wait_ge(gd[0], gd[1])
            cw.inc(g.dma_start(wslot(n, 5120), FFB[0]), 16)

        n_ffb1 = nblk
        nblk += 1
        wslot_guard.append((cffn.h, 2))

        @blk.gpsimd
        def _(g, n=n_ffb1):
            gd = wslot_guard[n - 3]
            g.wait_ge(gd[0], gd[1])
            cw.inc(g.dma_start(wslot(n, 5120), FFB[1]), 16)

        @blk.gpsimd
        def _(g):
            g.wait_ge(chh.h, 64)
            ccon.inc(g.dma_start(M1S[:, :], M1D), 16)
            ccon.inc(g.dma_start(M2S[:, :], M2D), 16)
        m_ready = ccon.v

        @blk.tensor
        def _(t, n=n_ffb0):
            t.wait_ge(cw.h, 16 * (n + 1))
            for ct in range(32):
                t.wait_ge(cx1.h, x1v_base + ct + 1)
                t.matmul(PSB[:, 0:512],
                         wslot(n, 5120)[:, 128 * ct: 128 * ct + 128],
                         REG2[:, 1024 * ct: 1024 * ct + 512],
                         start=(ct == 0), stop=False)

        @blk.tensor
        def _(t, n=n_ffb1):
            t.wait_ge(cw.h, 16 * (n + 1))
            for ct in range(32):
                t.matmul(PSB[:, 512:1024],
                         wslot(n, 5120)[:, 128 * ct: 128 * ct + 128],
                         REG2[:, 1024 * ct: 1024 * ct + 512],
                         start=(ct == 0), stop=False)

        @blk.tensor
        def _(t, n=n_art, x1_done=x1_done):
            t.wait_ge(cw.h, 16 * (n + 1))
            t.wait_ge(cx1.h, x1_done)
            for half in range(2):
                for ct in range(32):
                    mm = t.matmul(
                        PSA[0:80, 512 * half: 512 * half + 512],
                        wslot(n, 2560)[:, 80 * ct: 80 * ct + 80],
                        REG2[:, 1024 * ct + 512 * half: 1024 * ct + 512 * half + 512],
                        start=(ct == 0), stop=(ct == 31),
                    )
            cart.inc(mm)
        art_mm = cart.v

        @blk.scalar
        def _(sc, art_mm=art_mm):
            sc.wait_ge(cart.h, art_mm)
            sc.copy(RH[:, :], PSA[0:64, 0:1024])
            cart.inc(sc.copy(LGT[:, :], PSA[64:80, 0:1024]))
        art_ev = cart.v

        @blk.tensor
        def _(t, art_ev=art_ev):
            t.wait_ge(cart.h, art_ev)
            t.wait_ge(csyc.h, csyc_idf)
            for tt in range(8):
                mm = t.matmul(PSA[:, 1024 + 16 * tt: 1024 + 16 * tt + 16],
                              LGT[:, 128 * tt: 128 * tt + 128],
                              IDF[0:16, 0:16], start=True, stop=True)
            crt.inc(mm)
        lt_done = crt.v

        @blk.vector
        def _(v, lt_done=lt_done):
            v.wait_ge(crt.h, lt_done)
            v.tensor_copy(LGtm[:, 0:128], PSA[:, 1024:1152])
            v.drain()
            for tt in range(8):
                v.reduce_max(SCL[:, tt: tt + 1], LGtm[:, 16 * tt: 16 * tt + 16], axis=AX.X)
            v.drain()
            crte.inc(v.tensor_scalar(GTMP[:, 0:8], SCL[:, 0:8], -1.0, None,
                                     op0=AluOpType.mult))
        negmax_done = crte.v

        @blk.scalar
        def _(sc, negmax_done=negmax_done):
            sc.wait_ge(crte.h, negmax_done)
            for tt in range(8):
                act = sc.activation(ETM8[:, 16 * tt: 16 * tt + 16],
                                    LGtm[:, 16 * tt: 16 * tt + 16],
                                    AF.Exp, bias=GTMP[:, tt: tt + 1])
            csq.inc(act)
        exp8_done = csq.v

        @blk.vector
        def _(v, exp8_done=exp8_done):
            v.wait_ge(csq.h, exp8_done)
            for tt in range(8):
                v.reduce_sum(SCL8[:, tt: tt + 1], ETM8[:, 16 * tt: 16 * tt + 16], axis=AX.X)
            v.drain()
            v.reciprocal(SCL8[:, 8:16], SCL8[:, 0:8])
            v.drain()
            for tt in range(8):
                v.tensor_scalar(PRB[:, 16 * tt: 16 * tt + 16], ETM8[:, 16 * tt: 16 * tt + 16],
                                SCL8[:, 8 + tt: 9 + tt], None, op0=AluOpType.mult)
            v.drain()
            for tt in range(8):
                v.reduce_max(SCL8[:, 16 + tt: 17 + tt], PRB[:, 16 * tt: 16 * tt + 16], axis=AX.X)
            v.drain()
            for tt in range(8):
                v.tensor_scalar(MSX[:, 16 * tt: 16 * tt + 16], PRB[:, 16 * tt: 16 * tt + 16],
                                SCL8[:, 16 + tt: 17 + tt], None, op0=AluOpType.is_ge)
            v.drain()
            v.tensor_tensor(MSX[:, :], PRB[:, :], MSX[:, :], op=AluOpType.subtract)
            v.drain()
            for tt in range(8):
                v.reduce_max(SCL8[:, 24 + tt: 25 + tt], MSX[:, 16 * tt: 16 * tt + 16], axis=AX.X)
            v.drain()
            v.tensor_tensor(SCL8[:, 32:40], SCL8[:, 16:24], SCL8[:, 24:32], op=AluOpType.add)
            v.drain()
            v.tensor_scalar(SCL8[:, 32:40], SCL8[:, 32:40], 1e-6, None, op0=AluOpType.add)
            v.drain()
            v.reciprocal(SCL8[:, 40:48], SCL8[:, 32:40])
            v.drain()
            v.tensor_scalar(SCL8[:, 40:48], SCL8[:, 40:48], 0.5, None, op0=AluOpType.mult)
            v.drain()
            for tt in range(8):
                v.tensor_scalar(MSX[:, 16 * tt: 16 * tt + 16], PRB[:, 16 * tt: 16 * tt + 16],
                                SCL8[:, 24 + tt: 25 + tt], None, op0=AluOpType.is_ge)
            v.drain()
            v.tensor_tensor(PRB[:, :], PRB[:, :], MSX[:, :], op=AluOpType.mult)
            v.drain()
            for tt in range(8):
                mm = v.tensor_scalar(CMBtm[:, 16 * tt: 16 * tt + 16], PRB[:, 16 * tt: 16 * tt + 16],
                                     SCL8[:, 40 + tt: 41 + tt], None, op0=AluOpType.mult)
            crte.inc(mm)
        cmb_done = crte.v

        @blk.tensor
        def _(t, cmb_done=cmb_done):
            t.wait_ge(crte.h, cmb_done)
            for tt in range(8):
                mm = t.matmul(PSA[0:16, 128 * tt: 128 * tt + 128],
                              CMBtm[:, 16 * tt: 16 * tt + 16],
                              IDF[:, :], start=True, stop=True)
            crt.inc(mm)
        cmbT_done = crt.v

        @blk.vector
        def _(v, cmbT_done=cmbT_done):
            v.wait_ge(crt.h, cmbT_done)
            crte.inc(v.tensor_copy(CMB[:, :], PSA[0:16, 0:1024]))
        router_done = crte.v

        for j in range(8):
            @blk.tensor
            def _(t, j=j, router_done=router_done):
                t.wait_ge(crte.h, router_done)
                t.wait_ge(ccon.h, m_ready)
                if j >= 1:
                    t.wait_ge(ccte.h, j)
                for half in range(2):
                    t.matmul(PSA[:, 1024 + 512 * half: 1024 + 512 * half + 512],
                             M1S[:, 128 * j: 128 * j + 128],
                             CMB[:, 512 * half: 512 * half + 512],
                             start=True, stop=True)
                    mm = t.matmul(PSA[:, 512 * half: 512 * half + 512],
                                  M2S[:, :],
                                  RH[:, 512 * half: 512 * half + 512],
                                  start=True, stop=True)
                cct.inc(mm)
            myj = cct.v

            @blk.vector
            def _(v, j=j, myj=myj):
                v.wait_ge(cct.h, myj)
                for half in range(2):
                    v.tensor_copy(M2SB[:, :], PSA[:, 1024 + 512 * half: 1024 + 512 * half + 512])
                    mm = v.tensor_tensor(
                        CT[:, 1024 * j + 512 * half: 1024 * j + 512 * half + 512],
                        M2SB[:, :], PSA[:, 512 * half: 512 * half + 512],
                        op=AluOpType.mult)
                ccte.inc(mm)
        ct_done = ccte.v
        if dump:
            @blk.sync
            def _(s_):
                s_.wait_ge(ccte.h, ct_done)
                cosp.inc(s_.dma_start(DBG5, CT[:, :]), 16)
                cosp.inc(s_.dma_start(DBG6, RH[:, :]), 16)
                cosp.inc(s_.dma_start(DBG7, LGT[:, :]), 16)
                cosp.inc(s_.dma_start(DBG8, CMB[:, :]), 16)

        ct_base = ct_done - 8
        for rnd12, nslot in ((1, n_ffb0), (2, n_ffb1)):
            @blk.tensor
            def _(t, n=nslot, b_off=512 * (rnd12 - 1), ct_base=ct_base):
                for j in range(8):
                    t.wait_ge(ccte.h, ct_base + j + 1)
                    mm = t.matmul(PSB[:, b_off: b_off + 512],
                                  wslot(n, 5120)[:, 4096 + 128 * j: 4096 + 128 * j + 128],
                                  CT[:, 1024 * j: 1024 * j + 512],
                                  start=False, stop=(j == 7))
                cffn.inc(mm)

        ffn_rounds = 0

        def emit_ffn_epilogue(ot, tc, rnd):
            rr = (rnd - 1) % 2

            myffn = cffn.v

            @blk.vector
            def _(v, ot=ot, tc=tc, b_off=512 * rr, myffn=myffn):
                v.wait_ge(cffn.h, myffn)
                v.wait_ge(csyc.h, bb_ready)
                mm = v.scalar_tensor_tensor(
                    REG1[:, 1024 * ot + 512 * tc: 1024 * ot + 512 * tc + 512],
                    PSB[:, b_off: b_off + 512],
                    BBS[:, ot: ot + 1],
                    REG2[:, 1024 * ot + 512 * tc: 1024 * ot + 512 * tc + 512],
                    op0=AluOpType.add, op1=AluOpType.add)
                ch2.inc(mm)

            @blk.vector
            def _(v, ot=ot, tc=tc, rnd=rnd, rr=rr):
                if rnd > 2:
                    v.wait_ge(cstm.h, 64 + rnd - 2)
                sl = REG1[:, 1024 * ot + 512 * tc: 1024 * ot + 512 * tc + 512]
                csqv.inc(v.tensor_tensor(
                    SQ[:, 512 * rr: 512 * rr + 512], sl, sl, op=AluOpType.mult))

            mysq = csqv.v

            def stats_item(ot=ot, tc=tc, mysq=mysq, rr=rr):
                @blk.tensor
                def _(t):
                    t.wait_ge(csqv.h, mysq)
                    t.matmul(PSA[:, st_sum[tc]: st_sum[tc] + 512], ONE4K[:, :],
                             REG1[:, 1024 * ot + 512 * tc: 1024 * ot + 512 * tc + 512],
                             start=(ot == 0), stop=(ot == 31))
                    mm = t.matmul(PSA[:, st_sq[tc]: st_sq[tc] + 512], ONE4K[:, :],
                                  SQ[:, 512 * rr: 512 * rr + 512],
                                  start=(ot == 0), stop=(ot == 31))
                    cstm.inc(mm)
            pending_stats.append(stats_item)

        out_n = [0]

        def emit_out(ot, tc, myfin):
            out_n[0] += 1
            n = out_n[0]
            oslot = (n - 1) % 4

            @blk.vector
            def _(v, ot=ot, tc=tc, myfin=myfin, oslot=oslot, n=n):
                v.wait_ge(cfin.h, myfin)
                if n > 4:
                    v.wait_ge(cout.h, 16 * (n - 4))
                osl = OST[:, 512 * oslot: 512 * oslot + 512]
                v.tensor_tensor(osl,
                                REG1[:, 1024 * ot + 512 * tc: 1024 * ot + 512 * tc + 512],
                                RSTD2[:, 512 * tc: 512 * tc + 512], op=AluOpType.mult)
                cx1.inc(v.tensor_tensor(
                    osl, osl, BT2[:, 512 * tc: 512 * tc + 512], op=AluOpType.subtract))

            myo = cx1.v

            @blk.sync
            def _(s_, ot=ot, tc=tc, oslot=oslot, myo=myo):
                s_.wait_ge(cx1.h, myo)
                cout.inc(s_.dma_start(
                    OUT[128 * ot: 128 * ot + 128, 512 * tc: 512 * tc + 512],
                    OST[:, 512 * oslot: 512 * oslot + 512]), 16)

        for rnd12 in (1, 2):
            ffn_rounds += 1
            emit_ffn_epilogue(rnd12 - 1, 0, rnd12)

        norm_q = []
        for sweep_tc in range(2):
            for ot in (range(2, 32) if sweep_tc == 0 else range(32)):
                n = nblk
                nblk += 1
                ffn_rounds += 1
                rnd = ffn_rounds
                wslot_guard.append((cffn.h, rnd))

                @blk.gpsimd
                def _(g, n=n, ot=ot):
                    gd = wslot_guard[n - 3]
                    g.wait_ge(gd[0], gd[1])
                    cw.inc(g.dma_start(wslot(n, 5120), FFB[ot]), 16)

                b_off = 512 * ((rnd - 1) % 2)

                @blk.tensor
                def _(t, n=n, tc=sweep_tc, b_off=b_off, rnd=rnd):
                    t.wait_ge(cw.h, 16 * (n + 1))
                    t.wait_ge(ch2.h, rnd - 2)
                    for ct in range(32):
                        t.matmul(PSB[:, b_off: b_off + 512],
                                 wslot(n, 5120)[:, 128 * ct: 128 * ct + 128],
                                 REG2[:, 1024 * ct + 512 * tc: 1024 * ct + 512 * tc + 512],
                                 start=(ct == 0), stop=False)
                    for j in range(8):
                        mm = t.matmul(PSB[:, b_off: b_off + 512],
                                      wslot(n, 5120)[:, 4096 + 128 * j: 4096 + 128 * j + 128],
                                      CT[:, 1024 * j + 512 * tc: 1024 * j + 512 * tc + 512],
                                      start=False, stop=(j == 7))
                    cffn.inc(mm)
                flush_stats()
                emit_ffn_epilogue(ot, sweep_tc, rnd)
                for _i in range(2):
                    if norm_q:
                        norm_q.pop(0)()

            flush_stats(0)
            emit_ln_finalize(sweep_tc, cstm.v)
            myfin = cfin.v
            for oto in range(32):
                norm_q.append(lambda oto=oto, tc=sweep_tc, myfin=myfin:
                              emit_out(oto, tc, myfin))

        while norm_q:
            norm_q.pop(0)()

        if dump:
            @blk.sync
            def _(s_):
                s_.wait_ge(ch2.h, ch2.v)
                cosp.inc(s_.dma_start(DBG9, REG1[:, 0:32768]), 16)
                cosp.inc(s_.dma_start(RD1, RDB1[:, :]), 16)
                cosp.inc(s_.dma_start(RD2, RDB2[:, :]), 16)
                cosp.inc(s_.dma_start(RD3, RDB3[:, :]), 16)

        @blk.sync
        def _(s_):
            s_.wait_ge(cout.h, cout.v)
            s_.wait_ge(coutg.h, coutg.v)

        p4.close()
        p3.close()

    return nc


def _host_prep(inputs):
    f32 = np.float32
    x = np.ascontiguousarray(np.asarray(inputs["L0_attn_x"], f32))
    wq = np.asarray(inputs["L0_attn_wq"], f32)
    wk = np.asarray(inputs["L0_attn_wk"], f32)
    wv = np.asarray(inputs["L0_attn_wv"], f32)
    wo = np.asarray(inputs["L0_attn_wo"], f32)
    bw = np.asarray(inputs["L0_ffn_base_w"], f32)
    bb = np.asarray(inputs["L0_ffn_base_b"], f32)
    A = np.asarray(inputs["L0_ffn_A"], f32)
    Bm = np.asarray(inputs["L0_ffn_B"], f32)
    rw = np.asarray(inputs["L0_ffn_router_w"], f32)

    WQ_h = np.ascontiguousarray(
        wq.reshape(32, 128, 32, 128).transpose(2, 1, 0, 3)).reshape(32, 128, 4096)
    WK_h = np.ascontiguousarray(
        wk.reshape(32, 128, 8, 128).transpose(2, 1, 0, 3)).reshape(8, 128, 4096)
    WV_h = np.ascontiguousarray(
        wv.reshape(32, 128, 8, 128).transpose(2, 1, 0, 3)).reshape(8, 128, 4096)
    WKV_h = np.ascontiguousarray(np.concatenate([WK_h, WV_h], 0))
    WO_h = np.ascontiguousarray(
        wo.reshape(32, 128, 32, 128).transpose(2, 1, 0, 3)).reshape(32, 128, 4096)
    BWT_h = np.ascontiguousarray(
        bw.reshape(32, 128, 32, 128).transpose(0, 3, 2, 1)).reshape(32, 128, 4096)
    BF_h = np.ascontiguousarray(
        Bm.transpose(0, 2, 1).reshape(8, 128, 32, 128).transpose(2, 1, 0, 3)
    ).reshape(32, 128, 1024)
    FFB_h = np.ascontiguousarray(np.concatenate([BWT_h, BF_h], axis=2))
    art = np.concatenate([A.T, rw.T], axis=1)
    ART_h = np.ascontiguousarray(
        art.reshape(32, 128, 80).transpose(1, 0, 2)).reshape(128, 2560)

    M1_h = np.zeros((16, 1024), f32)
    pp = np.arange(1024)
    M1_h[2 * (pp // 128) + (pp % 128) // 64, pp] = 1.0
    M2_h = np.zeros((64, 128), f32)
    M2_h[np.arange(128) % 64, np.arange(128)] = 1.0
    BB_h = np.ascontiguousarray(bb.reshape(32, 128).T)
    ID_h = np.eye(128, dtype=f32)

    bf = lambda a: np.ascontiguousarray(a.astype(BF16_NP))
    shared = dict(WKV=bf(WKV_h), WQ=bf(WQ_h), WOD=bf(WO_h), FFB=bf(FFB_h),
                  ARTW=bf(ART_h), M1D=M1_h, M2D=M2_h, BBD=BB_h, IDD=ID_h)

    in_maps = []
    meta = []
    for b in range(4):
        xT = np.ascontiguousarray(x[b].T)
        XT_h = bf(np.ascontiguousarray(
            xT.reshape(32, 128, 4, 512).transpose(2, 1, 0, 3)))
        for role in range(2):
            ch = CHUNKS[role]
            cols = np.concatenate([np.arange(c * 256, (c + 1) * 256) for c in ch])
            xown = xT[:, cols]
            XOWN_h = np.ascontiguousarray(
                xown.reshape(32, 128, 1024).transpose(1, 0, 2))
            m4 = np.zeros((4, 4, 128, 256), f32)
            for s in range(4):
                q0 = ch[s] * 256
                for j in range(4):
                    kv0 = (4 * s + j) * 128
                    kvi = kv0 + np.arange(128)[:, None]
                    qi = q0 + np.arange(256)[None, :]
                    m4[s, j] = (kvi <= qi).astype(f32)
            MSK_h = np.ascontiguousarray(m4.transpose(2, 0, 1, 3)).reshape(128, 4096)
            im = dict(shared)
            im["XT"] = XT_h
            im["XOWN"] = XOWN_h
            im["MSK"] = bf(MSK_h)
            in_maps.append(im)
            meta.append((b, ch))
    return in_maps, meta


def kernel(**inputs):
    in_maps, meta = _host_prep(inputs)
    nc = build_program()
    res = run_bass_kernel_spmd(nc, in_maps, core_ids=list(range(8)))
    out = np.empty((4, 2048, 4096), np.float32)
    for i, r in enumerate(res.results):
        b, ch = meta[i]
        oT = np.asarray(r["OUT"])
        o = np.ascontiguousarray(oT.T)
        for k, c in enumerate(ch):
            out[b, c * 256:(c + 1) * 256] = o[k * 256:(k + 1) * 256]
    return out
